# version 10
# speedup vs baseline: 1.1799x; 1.0069x over previous
import contextlib
import functools
import math

import numpy as np
import ml_dtypes
import concourse.bass as bass
import concourse.mybir as mybir
from concourse.bass_utils import run_bass_kernel_spmd

F32 = mybir.dt.float32
BF16 = mybir.dt.bfloat16
I32 = mybir.dt.int32
AOT = mybir.AluOpType
AFT = mybir.ActivationFunctionType
AXL = mybir.AxisListType

NCORES = 8
B, S, D = 4, 4096, 1024
SLOC = S // NCORES
TOK = B * SLOC
TC = TOK // 128
DC = D // 128
H0, HD0 = 16, 64
NBIN = 513
EPS0 = 1e-6
MAGIC = 12582912.0

ENGINES = ("sync", "tensor", "vector", "scalar", "gpsimd")


class Sem:
    def __init__(self, handle):
        self.h = handle
        self.n = 0

    def inc(self, by=1):
        self.n += by
        return self.n


class Prog:

    def __init__(self, nc):
        self.nc = nc
        self.q = {e: [] for e in ENGINES}
        self._ctx = contextlib.ExitStack()
        self.SE = {e: Sem(self._ctx.enter_context(nc.semaphore(f"S_{e}")))
                   for e in ENGINES}
        self.dma = {}
        self.aux = {}

    def dma_sem(self, name):
        if name not in self.dma:
            self.dma[name] = Sem(self._ctx.enter_context(self.nc.semaphore(name)))
        return self.dma[name]

    def aux_sem(self, name):
        if name not in self.aux:
            self.aux[name] = Sem(self._ctx.enter_context(self.nc.semaphore(name)))
        return self.aux[name]

    def on(self, eng, fn):
        self.q[eng].append(fn)

    def wait_S(self, eng, tok):
        toks = tok if isinstance(tok, list) else [tok]
        for (e2, v) in toks:
            if e2 == eng:
                continue
            sem = self.SE[e2]
            self.on(eng, lambda e, h=sem.h, v=v: e.wait_ge(h, v))

    def signal_S(self, eng):
        sem = self.SE[eng]
        v = sem.inc()
        if eng == "sync":
            self.on(eng, lambda e, h=sem.h: e.nop().then_inc(h, 1))
        else:
            self.on(eng, lambda e, h=sem.h: e.drain().then_inc(h, 1))
        return (eng, v)

    def guard(self, eng):
        s = self.aux_sem("g_" + eng)
        v = s.inc()
        self.on(eng, lambda e: e.drain().then_inc(s.h, 1))
        self.on(eng, lambda e, v=v: e.wait_ge(s.h, v))

    def barrier(self, engines):
        toks = [self.signal_S(e) for e in engines]
        for e in engines:
            self.wait_S(e, toks)
        return toks

    def finalize(self, block):
        for e in ENGINES:
            fns = self.q[e]
            if not fns:
                continue
            def body(engine, fns=fns):
                for f in fns:
                    f(engine)
            getattr(block, e)(body)


def tmj(t, c, w=1024):
    return t[:, c * w:(c + 1) * w]


def emit_load_tokmajor(P, dst_sbuf, dram_ap, sem_name, nchunk=TC, width=D):
    s = P.dma_sem(sem_name)
    def f(e, s=s):
        e.dma_start(
            out=dst_sbuf[:, :nchunk * width].rearrange("p (c m) -> p c m", c=nchunk),
            in_=dram_ap.rearrange("(c p) m -> p c m", p=128),
        ).then_inc(s.h, 16)
    P.on("sync", f)
    return s.inc(16)


def emit_store_tokmajor(P, dram_ap, src_sbuf, sem_name, eng="gpsimd", nchunk=TC, width=D):
    s = P.dma_sem(sem_name)
    def f(e, s=s):
        e.dma_start(
            out=dram_ap.rearrange("(c p) m -> p c m", p=128),
            in_=src_sbuf[:, :nchunk * width].rearrange("p (c m) -> p c m", c=nchunk),
        ).then_inc(s.h, 16)
    P.on(eng, f)
    return s.inc(16)


def emit_ln(P, out_t, in_t, scr, mean_s, work, gamma=None, beta=None,
            eps=1e-5, nchunk=TC, wd=D, out_dtype_copy=None):
    inv = 1.0 / wd
    lnv = P.aux_sem("ln_v")
    lns = P.aux_sem("ln_s")
    means = mean_s[:, 0:nchunk]
    sq = mean_s[:, nchunk:2 * nchunk]
    rstd = mean_s[:, 2 * nchunk:3 * nchunk]

    for c in range(nchunk):
        def v1(e, c=c):
            e.tensor_reduce(means[:, c:c + 1], tmj(in_t, c, wd), AXL.X, AOT.add)
        P.on("vector", v1)
    P.guard("vector")
    P.on("vector", lambda e: e.tensor_scalar(means, means, inv, None, AOT.mult))
    P.guard("vector")
    for c in range(nchunk):
        def v2(e, c=c):
            e.tensor_scalar(tmj(out_t, c, wd), tmj(in_t, c, wd),
                            means[:, c:c + 1], None, AOT.subtract)
        P.on("vector", v2)
    t1 = lnv.inc()
    P.on("vector", lambda e: e.drain().then_inc(lnv.h, 1))
    P.on("scalar", lambda e, t1=t1: e.wait_ge(lnv.h, t1))
    for c in range(nchunk):
        def s1(e, c=c):
            e.activation(work[:, :wd], tmj(out_t, c, wd), AFT.Square,
                         accum_out=sq[:, c:c + 1])
        P.on("scalar", s1)
    P.guard("scalar")
    P.on("scalar", lambda e: e.activation(rstd, sq, AFT.Sqrt,
                                          bias=P.epscol[:, 0:1], scale=inv))
    t2 = lns.inc()
    P.on("scalar", lambda e: e.drain().then_inc(lns.h, 1))
    P.on("vector", lambda e, t2=t2: e.wait_ge(lns.h, t2))
    P.on("vector", lambda e: e.reciprocal(rstd, rstd))
    P.guard("vector")
    for c in range(nchunk):
        if gamma is None:
            def v4(e, c=c):
                e.tensor_scalar(tmj(out_t, c, wd), tmj(out_t, c, wd),
                                rstd[:, c:c + 1], None, AOT.mult)
            P.on("vector", v4)
        else:
            def v4(e, c=c):
                e.scalar_tensor_tensor(tmj(out_t, c, wd), tmj(out_t, c, wd),
                                       rstd[:, c:c + 1], gamma[:, :wd],
                                       AOT.mult, AOT.mult)
            P.on("vector", v4)
            def v5(e, c=c):
                e.tensor_tensor(tmj(out_t, c, wd), tmj(out_t, c, wd),
                                beta[:, :wd], AOT.add)
            P.on("vector", v5)


def emit_l0ffn_prep(P, T):
    nc = P.nc
    qT = T["qrhsT"]
    qr = T["qrhs"]
    rsb = T["rhs_scale_bcast"]
    scr = T["scr_a"]
    rbuf = T["scr_c"]
    qi = T["qi32"]
    ms = T["small"]
    ld = P.dma_sem("dRhs")
    cons = P.aux_sem("rhs_cons")
    cons_base = cons.n

    for c in range(DC):
        par = c % 2
        ldp = P.dma_sem(f"dRhsp{par}")
        if c >= 2:
            P.on("sync", lambda e, v=cons_base + c - 1: e.wait_ge(cons.h, v))

        def f(e, c=c, par=par, ldp=ldp):
            e.dma_start(
                out=rbuf[:, par * 1024:(par + 1) * 1024],
                in_=T["rhsT_dram"][c * 128:(c + 1) * 128, :],
            ).then_inc(ldp.h, 16)
        P.on("sync", f)
        nld = ldp.inc(16)
        xi = rbuf[:, par * 1024:(par + 1) * 1024]
        amax = ms[:, 2:3]
        rs = ms[:, 3:4]
        sc = ms[:, 8 + c:9 + c]

        def v1a(e, xi=xi, amax=amax, nld=nld, ldp=ldp):
            e.wait_ge(ldp.h, nld)
            e.tensor_reduce(amax, xi, AXL.X, AOT.max, apply_absolute_value=True)
        P.on("vector", v1a)
        P.guard("vector")
        def v1b(e, amax=amax, sc=sc):
            e.tensor_scalar(sc, amax, 1e-12, 1.0 / 127.0, AOT.max, AOT.mult)
        P.on("vector", v1b)
        P.guard("vector")
        def v1c(e, rs=rs, sc=sc):
            e.reciprocal(rs, sc)
        P.on("vector", v1c)
        P.guard("vector")
        def v1d(e, xi=xi, rs=rs, c=c):
            e.tensor_scalar(scr[:, :], xi, rs, None, AOT.mult)
            e.tensor_scalar(scr[:, :], scr[:, :], MAGIC, -MAGIC, AOT.add, AOT.add)
            e.tensor_scalar(scr[:, :], scr[:, :], 127.0, -127.0, AOT.min, AOT.max)
            e.tensor_copy(qi[:, :], scr[:, :])
            e.tensor_copy(tmj(qT, c), qi[:, :])
        P.on("vector", v1d)
        cons.inc()
        P.on("vector", lambda e: e.drain().then_inc(cons.h, 1))
    vq = P.signal_S("vector")

    P.wait_S("gpsimd", vq)
    db = P.dma_sem("dB")

    def g1(e, db=db):
        e.dma_start(out=T["rs_dram"].rearrange("c j -> j c"),
                    in_=ms[:, 8:16],
                    allow_slow_non_contiguous=True).then_inc(db.h, 16)
    P.on("gpsimd", g1)
    n1 = db.inc(16)
    P.on("gpsimd", lambda e, n=n1: e.wait_ge(db.h, n))

    def g2(e, db=db):
        e.dma_start(
            out=rsb[:, :].rearrange("p (a m) -> p a m", a=1),
            in_=T["rs_dram"].rearrange("c j -> (c j)").partition_broadcast(128),
        ).then_inc(db.h, 16)
    P.on("gpsimd", g2)
    n2 = db.inc(16)
    P.on("gpsimd", lambda e, n=n2: e.wait_ge(db.h, n))
    P.signal_S("gpsimd")

    sv = P.barrier(("tensor", "vector", "scalar", "sync", "gpsimd"))
    dt = P.dma_sem("dT")
    for cc in range(DC):
        for kc in range(DC):
            def f(e, cc=cc, kc=kc, dt=dt):
                e.dma_start(
                    out=qr[:, kc * D + cc * 128: kc * D + (cc + 1) * 128],
                    in_=qT[:, cc * D + kc * 128: cc * D + (kc + 1) * 128],
                    transpose=True,
                ).then_inc(dt.h, 16)
            P.on("sync", f)
    n = dt.inc(16 * DC * DC)
    P.on("sync", lambda e, n=n: e.wait_ge(P.dma["dT"].h, n))
    P.signal_S("sync")


def emit_l0ffn(P, T):
    x1 = T["x1"]
    qb = T["qbf"]
    qbT = T["qbfT"]
    lsc = T["lscale"]
    scr = T["scr_a"]
    scr2 = T["scr_b"]
    qi = T["qi32"]
    ms = T["small"]

    rs16 = ms[:, 48:48 + TC]
    for c in range(TC):
        def v0(e, c=c):
            e.tensor_reduce(ms[:, 64 + c:65 + c], tmj(x1, c), AXL.X, AOT.max,
                            apply_absolute_value=True)
        P.on("vector", v0)
    P.guard("vector")
    P.on("vector", lambda e: e.tensor_scalar(lsc[:, 0:TC], ms[:, 64:64 + TC],
                                             1e-12, 1.0 / 127.0, AOT.max, AOT.mult))
    P.guard("vector")
    P.on("vector", lambda e: e.reciprocal(rs16, lsc[:, 0:TC]))
    P.guard("vector")
    for c in range(TC):
        def v1(e, c=c):
            e.tensor_scalar(scr[:, :], tmj(x1, c), rs16[:, c:c + 1], None, AOT.mult)
            e.tensor_scalar(scr[:, :], scr[:, :], MAGIC, -MAGIC, AOT.add, AOT.add)
            e.tensor_scalar(scr[:, :], scr[:, :], 127.0, -127.0, AOT.min, AOT.max)
            e.tensor_copy(qi[:, :], scr[:, :])
            e.tensor_copy(tmj(qb, c), qi[:, :])
        P.on("vector", v1)
    vq = P.signal_S("vector")

    dt = P.dma_sem("dT")
    P.wait_S("sync", vq)
    for tc in range(TC):
        for kc in range(DC):
            def f(e, tc=tc, kc=kc, dt=dt):
                e.dma_start(
                    out=qbT[:, kc * TOK + tc * 128: kc * TOK + (tc + 1) * 128],
                    in_=qb[:, tc * D + kc * 128: tc * D + (kc + 1) * 128],
                    transpose=True,
                ).then_inc(dt.h, 16)
            P.on("sync", f)
    ndma = dt.inc(16 * TC * DC)
    P.on("sync", lambda e, n=ndma: e.wait_ge(P.dma["dT"].h, n))
    tdone = P.signal_S("sync")

    if "premm_taps" in T:
        P.wait_S("gpsimd", tdone)
        dd = P.dma_sem("dPre")
        for nm, src in T["premm_taps"]:
            def f(e, nm=nm, src=src, dd=dd):
                e.dma_start(out=nm, in_=src[:, :]).then_inc(dd.h, 16)
            P.on("gpsimd", f)
            dd.inc(16)
        P.on("gpsimd", lambda e, n=dd.n: e.wait_ge(dd.h, n))
        tdone = P.signal_S("gpsimd")

    mmS = P.aux_sem("l0f_mm")
    evS = P.aux_sem("l0f_ev")
    P.wait_S("tensor", tdone)
    pb = T["psum"]
    qrhs = T["qrhs"]
    rsb = T["rhs_scale_bcast"]
    ev_base = evS.n
    mm_base = mmS.n
    for i in range(TC * 2):
        tc, ch = divmod(i, 2)
        bank = i % 4
        ps = pb[bank]
        if i >= 4:
            P.on("tensor", lambda e, v=ev_base + i - 3: e.wait_ge(evS.h, v))
        for kc in range(DC):
            def mm(e, tc=tc, ch=ch, kc=kc, ps=ps):
                e.matmul(
                    ps[:, 0:512],
                    qbT[:, kc * TOK + tc * 128: kc * TOK + (tc + 1) * 128],
                    qrhs[:, kc * D + ch * 512: kc * D + (ch + 1) * 512],
                    start=(kc == 0), stop=(kc == DC - 1),
                )
            P.on("tensor", mm)
        mv = mmS.inc()
        P.on("tensor", lambda e: e.drain().then_inc(mmS.h, 1))

        def ev(e, tc=tc, ch=ch, ps=ps, mv=mv):
            e.wait_ge(mmS.h, mv)
            sl = tmj(x1, tc)[:, ch * 512:(ch + 1) * 512]
            if "acc_dump" in T:
                e.tensor_copy(tmj(T["acc_dump"], tc)[:, ch * 512:(ch + 1) * 512],
                              ps[:, 0:512])
            e.scalar_tensor_tensor(scr2[:, 0:512], ps[:, 0:512],
                                   lsc[:, tc:tc + 1],
                                   rsb[:, ch * 512:(ch + 1) * 512],
                                   AOT.mult, AOT.mult)
            e.tensor_tensor(sl, sl, scr2[:, 0:512], AOT.add)
        P.on("vector", ev)
        evS.inc()
        P.on("vector", lambda e: e.drain().then_inc(evS.h, 1))
    P.barrier(("tensor", "vector", "scalar"))


def emit_l0ffn_fused(P, T):
    nc = P.nc
    pb = T["psum"]
    x1 = T["x1"]
    acc = T["acc"]
    qr = T["qrhs"]
    bt = T["bt"]
    scr = T["scr_a"]
    qi = T["qi32"]
    ms = T["small"]
    rsb = T["rhs_scale_bcast"]
    lsc = T["lscale"]
    mmS = P.aux_sem("l0f_mm")
    evS = P.aux_sem("l0f_ev")
    evC = P.aux_sem("l0f_evc")
    cons = P.aux_sem("rhs_cons")

    rs16 = ms[:, 48:48 + TC]
    for c in range(TC):
        def v0(e, c=c):
            e.tensor_reduce(ms[:, 64 + c:65 + c], tmj(x1, c), AXL.X, AOT.max,
                            apply_absolute_value=True)
        P.on("vector", v0)
    P.guard("vector")
    P.on("vector", lambda e: e.tensor_scalar(lsc[:, 0:TC], ms[:, 64:64 + TC],
                                             1e-12, 1.0 / 127.0, AOT.max, AOT.mult))
    P.guard("vector")
    P.on("vector", lambda e: e.reciprocal(rs16, lsc[:, 0:TC]))
    P.guard("vector")

    first_bcast = [True]
    for p in range(4):
        tr_mm = {}
        for c in range(DC):
            par = c % 2
            ldp = P.dma_sem(f"dRhsp{par}")
            if not (p == 0 and c < 2):
                P.on("sync", lambda e, v=cons.n: e.wait_ge(cons.h, v))
            def f(e, c=c, par=par, ldp=ldp):
                e.dma_start(out=T["rbuf"][:, par * 1024:(par + 1) * 1024],
                            in_=T["rhsT_dram"][c * 128:(c + 1) * 128, :]
                            ).then_inc(ldp.h, 16)
            P.on("sync", f)
            nld = ldp.inc(16)
            xi = T["rbuf"][:, par * 1024:(par + 1) * 1024]
            sc = ms[:, 8 + c:9 + c]
            rcp = ms[:, 16 + c:17 + c]
            def v1a(e, xi=xi, nld=nld, ldp=ldp, c=c, p=p, sc=sc, rcp=rcp,
                    tv=None if True else 0, **_kw):
                e.wait_ge(ldp.h, nld)
                if tv is not None:
                    e.wait_ge(mmS.h, tv)
                if p == 0:
                    e.tensor_reduce(ms[:, 2:3], xi, AXL.X, AOT.max,
                                    apply_absolute_value=True)
            P.on("vector", functools.partial(
                v1a, tv=(tr_mm[c - 1] if c >= 1 else None)))
            if p == 0:
                P.guard("vector")
                P.on("vector", lambda e, sc=sc:
                     e.tensor_scalar(sc, ms[:, 2:3], 1e-12, 1.0 / 127.0,
                                     AOT.max, AOT.mult))
                P.guard("vector")
                P.on("vector", lambda e, rcp=rcp, sc=sc: e.reciprocal(rcp, sc))
                P.guard("vector")
            def v1d(e, xi=xi, rcp=rcp):
                for hf in range(2):
                    s = scr[:, hf * 512:(hf + 1) * 512]
                    e.tensor_scalar(s, xi[:, hf * 512:(hf + 1) * 512], rcp,
                                    None, AOT.mult)
                    e.tensor_scalar(s, s, MAGIC, -MAGIC, AOT.add, AOT.add)
                    e.tensor_scalar(s, s, 127.0, -127.0, AOT.min, AOT.max)
                    e.tensor_copy(qi[:, 0:512], s)
                    e.tensor_copy(s, qi[:, 0:512])
            P.on("vector", v1d)
            cons.inc()
            vv = evS.inc()
            P.on("vector", lambda e: e.drain().then_inc(cons.h, 1))
            P.on("vector", lambda e: e.drain().then_inc(evS.h, 1))
            P.on("tensor", lambda e, v=vv: e.wait_ge(evS.h, v))
            if c >= 1:
                P.on("tensor", lambda e, v=tr_mm["ev", c - 1]: e.wait_ge(evS.h, v))
            tr_mm[c] = mmS.inc()
            for k2 in range(2):
                def tr(e, k2=k2, p=p):
                    m = e.transpose(pb[6][:, k2 * 128:(k2 + 1) * 128],
                                    scr[:, (2 * p + k2) * 128:(2 * p + k2 + 1) * 128],
                                    T["ident"][:])
                    if k2 == 1:
                        m.then_inc(mmS.h, 1)
                P.on("tensor", tr)
            def cp(e, c=c, mv=tr_mm[c]):
                e.wait_ge(mmS.h, mv)
                e.tensor_copy(
                    qr.rearrange("p (k m) -> p k m", k=2)[:, :, c * 128:(c + 1) * 128],
                    pb[6][:, 0:256].rearrange("p (j m) -> p j m", j=2))
            P.on("vector", cp)
            tr_mm["ev", c] = evS.inc()
            P.on("vector", lambda e: e.drain().then_inc(evS.h, 1))
        vq = P.signal_S("vector")

        if p == 0:
            P.wait_S("gpsimd", vq)
            db = P.dma_sem("dB")
            def g1(e, db=db):
                e.dma_start(out=T["rs_dram"].rearrange("c j -> j c"), in_=ms[:, 8:16],
                            allow_slow_non_contiguous=True).then_inc(db.h, 16)
            P.on("gpsimd", g1)
            P.on("gpsimd", lambda e, n=db.inc(16): e.wait_ge(db.h, n))
            def g2(e, db=db):
                e.dma_start(
                    out=rsb[:, :].rearrange("p (a m) -> p a m", a=1),
                    in_=T["rs_dram"].rearrange("c j -> (c j)").partition_broadcast(128),
                ).then_inc(db.h, 16)
            P.on("gpsimd", g2)
            P.on("gpsimd", lambda e, n=db.inc(16): e.wait_ge(db.h, n))
            P.signal_S("gpsimd")
        P.barrier(("vector", "tensor", "scalar", "sync", "gpsimd"))

        lq = {}
        for tc in range(TC):
            par = tc % 2
            qscr = T["rbuf"][:, par * 1024:(par + 1) * 1024]
            def vq1(e, tc=tc, qscr=qscr,
                    tv=None, **_kw):
                if tv is not None:
                    e.wait_ge(mmS.h, tv)
                for hf in range(2):
                    s = qscr[:, hf * 512:(hf + 1) * 512]
                    e.tensor_scalar(s, tmj(x1, tc)[:, hf * 512:(hf + 1) * 512],
                                    rs16[:, tc:tc + 1], None, AOT.mult)
                    e.tensor_scalar(s, s, MAGIC, -MAGIC, AOT.add, AOT.add)
                    e.tensor_scalar(s, s, 127.0, -127.0, AOT.min, AOT.max)
                    e.tensor_copy(qi[:, 0:512], s)
                    e.tensor_copy(s, qi[:, 0:512])
            P.on("vector", functools.partial(
                vq1, tv=(lq["tr", tc - 2] if tc >= 2 else None)))
            vv = evS.inc()
            P.on("vector", lambda e: e.drain().then_inc(evS.h, 1))
            P.on("tensor", lambda e, v=vv: e.wait_ge(evS.h, v))
            if tc >= 1:
                P.on("tensor", lambda e, v=lq["cp", tc - 1]: e.wait_ge(evC.h, v))
            lq["tr", tc] = mmS.inc()
            for k2 in range(2):
                def tr(e, k2=k2, p=p, qscr=qscr):
                    m = e.transpose(pb[7][:, k2 * 128:(k2 + 1) * 128],
                                    qscr[:, (2 * p + k2) * 128:(2 * p + k2 + 1) * 128],
                                    T["ident"][:])
                    if k2 == 1:
                        m.then_inc(mmS.h, 1)
                P.on("tensor", tr)
            def cp(e, tc=tc, mv=lq["tr", tc],
                   pv=None, **_kw):
                e.wait_ge(mmS.h, mv)
                if pv is not None:
                    e.wait_ge(mmS.h, pv)
                e.copy(bt[:, 0:256], pb[7][:, 0:256])
            P.on("scalar", functools.partial(
                cp, pv=(lq["mm", tc - 1] if tc >= 1 else None)))
            lq["cp", tc] = evC.inc()
            P.on("scalar", lambda e: e.drain().then_inc(evC.h, 1))
            for ch in range(2):
                ps = pb[par * 2 + ch]
                P.on("tensor", lambda e, v=lq["cp", tc]: e.wait_ge(evC.h, v))
                if tc >= 2:
                    P.on("tensor", lambda e, v=lq["dq", tc - 2, ch]: e.wait_ge(evS.h, v))
                for k2 in range(2):
                    def mm(e, ch=ch, k2=k2, ps=ps, fin=(ch == 1)):
                        m = e.matmul(ps[:, 0:512], bt[:, k2 * 128:(k2 + 1) * 128],
                                     qr[:, k2 * 1024 + ch * 512: k2 * 1024 + (ch + 1) * 512],
                                     start=(k2 == 0), stop=(k2 == 1))
                        if fin and k2 == 1:
                            m.then_inc(mmS.h, 1)
                    P.on("tensor", mm)
            lq["mm", tc] = mv = mmS.inc()
            for ch in range(2):
                ps = pb[par * 2 + ch]
                def ev(e, tc=tc, ch=ch, ps=ps, mv=mv, p=p):
                    e.wait_ge(mmS.h, mv)
                    asl = tmj(acc, tc)[:, ch * 512:(ch + 1) * 512]
                    if p == 0:
                        e.tensor_copy(asl, ps[:, 0:512])
                    else:
                        e.tensor_tensor(asl, asl, ps[:, 0:512], AOT.add)
                    if p == 3:
                        sl = tmj(x1, tc)[:, ch * 512:(ch + 1) * 512]
                        e.scalar_tensor_tensor(T["scr_b"][:, 0:512], asl,
                                               lsc[:, tc:tc + 1],
                                               rsb[:, ch * 512:(ch + 1) * 512],
                                               AOT.mult, AOT.mult)
                        e.tensor_tensor(sl, sl, T["scr_b"][:, 0:512], AOT.add)
                P.on("vector", ev)
                lq["dq", tc, ch] = evS.inc()
                P.on("vector", lambda e: e.drain().then_inc(evS.h, 1))
        P.barrier(("tensor", "vector", "scalar", "sync"))
    P.barrier(("tensor", "vector", "scalar"))


def emit_l0attn(P, T, x_dram, xT_dram, out_dram):
    nc = P.nc
    pb = T["psum"]
    QA, KA, VA, VA2 = T["QA"], T["KA"], T["VA"], T["VA2"]
    scrD = T["scrD"]
    si_rows = VA2[0:16, 0:2048]
    sa_rows = VA2[0:16, 2048:4096]
    zrow = T["kvt"][0:1, 0:992]
    ktile = T["kvt"][:, 0:512]
    vtile = T["kvt"][:, 512:1024]
    ksum4, qsum4 = T["ksum4"], T["qsum4"]
    bd = T["bd"]
    eps416 = T["eps416"]
    epsT = T["epsT"]
    mmS = P.aux_sem("l0a_mm")
    evSs = P.aux_sem("l0a_evs")
    evSv = P.aux_sem("l0a_evv")
    agS = P.aux_sem("l0a_ag")
    vcons = P.aux_sem("l0a_vcons")

    def tap(name, src_ap, tok):
        taps = T.get("l0a_taps", {})
        if name not in taps:
            return None
        P.wait_S("gpsimd", tok)
        d = P.dma_sem("dL0Tap")
        def g(e, d=d, src_ap=src_ap, name=name):
            e.dma_start(out=taps[name], in_=src_ap).then_inc(d.h, 16)
        P.on("gpsimd", g)
        P.on("gpsimd", lambda e, n=d.inc(16): e.wait_ge(d.h, n))
        return P.signal_S("gpsimd")

    dXT = P.dma_sem("dL0XT")
    dVs = P.dma_sem("dL0Vsp")
    grp_ev = {}
    grp_mm = {}
    vtile_dma = {}
    gidx = 0
    for b in range(B):
        if b >= 1:
            P.on("sync", lambda e, v=grp_mm[(b - 1) * DC + DC - 1]:
                 e.wait_ge(mmS.h, v))
        def fxt(e, b=b, dXT=dXT):
            e.dma_start(
                out=VA2.rearrange("p (c t) -> p c t", c=DC),
                in_=xT_dram[:, b * SLOC:(b + 1) * SLOC]
                    .rearrange("(c p) t -> p c t", p=128),
            ).then_inc(dXT.h, 16)
        P.on("sync", fxt)
        nxt = dXT.inc(16)
        for oc in range(DC):
            par = oc % 2
            g = gidx
            gidx += 1
            dW = P.dma_sem(f"dL0Wp{par}")
            if g >= 2:
                P.on("sync", lambda e, v=grp_mm[g - 2]: e.wait_ge(mmS.h, v))
            wq = VA[:, par * 1024:(par + 1) * 1024]
            wk = VA[:, 2048 + par * 1024:2048 + (par + 1) * 1024]
            wv = scrD[:, par * 1024:(par + 1) * 1024]
            def fw(e, oc=oc, wq=wq, wk=wk, wv=wv, dW=dW):
                for dst, src in ((wq, T["l0a_wq_d"]), (wk, T["l0a_wk_d"]),
                                 (wv, T["l0a_wv_d"])):
                    e.dma_start(
                        out=dst.rearrange("p (c m) -> p c m", c=DC),
                        in_=src[oc].rearrange("(c p) m -> p c m", p=128),
                    ).then_inc(dW.h, 16)
            P.on("sync", fw)
            nw = dW.inc(48)
            P.on("tensor", lambda e, n=nw, dW=dW: e.wait_ge(dW.h, n))
            if oc == 0:
                P.on("tensor", lambda e, n=nxt: e.wait_ge(dXT.h, n))
            if g >= 2:
                vs, vv = grp_ev[g - 2]
                P.on("tensor", lambda e, v=vs: e.wait_ge(evSs.h, v))
                P.on("tensor", lambda e, v=vv: e.wait_ge(evSv.h, v))
            mv1 = mmS.inc()
            grp_mm[g] = mv1
            for wi, (w_, bank) in enumerate(
                    ((wq, pb[par]), (wk, pb[2 + par]), (wv, pb[4 + par]))):
                for dc in range(DC):
                    def mm(e, w_=w_, dc=dc, ps=bank, fin=(wi == 2 and True)):
                        m = e.matmul(ps[:, 0:512], w_[:, dc * 128:(dc + 1) * 128],
                                     VA2[:, dc * 512:(dc + 1) * 512],
                                     start=(dc == 0), stop=(dc == DC - 1))
                        if fin and dc == DC - 1:
                            m.then_inc(mmS.h, 1)
                    P.on("tensor", mm)
            def evq(e, b=b, oc=oc, ps=pb[par], mv=mv1):
                e.wait_ge(mmS.h, mv)
                e.activation(QA[:, oc * 2048 + b * 512: oc * 2048 + (b + 1) * 512],
                             ps[:, 0:512], AFT.Sigmoid,
                             bias=T["bq_sb"][:, oc:oc + 1])
            P.on("scalar", evq)
            def evk(e, oc=oc, ps=pb[2 + par]):
                e.activation(ktile[:, :], ps[:, 0:512], AFT.Sigmoid,
                             bias=T["bk_sb"][:, oc:oc + 1])
            P.on("scalar", evk)
            sv1 = evSs.inc()
            P.on("scalar", lambda e: e.drain().then_inc(evSs.h, 1))
            def evv(e, oc=oc, ps=pb[4 + par], mv=mv1, g=g):
                e.wait_ge(mmS.h, mv)
                if g >= 1:
                    e.wait_ge(dVs.h, vtile_dma[g - 1])
                e.tensor_scalar(vtile[:, :], ps[:, 0:512],
                                T["bv_sb"][:, oc:oc + 1], None, AOT.add)
            P.on("vector", evv)
            def red(e, b=b, oc=oc, sv=sv1):
                e.wait_ge(evSs.h, sv)
                e.tensor_reduce(T["ksum_cols"][:, oc * 4 + b:oc * 4 + b + 1],
                                ktile[:, :], AXL.X, AOT.add)
                e.tensor_reduce(T["qsum_cols"][:, oc * 4 + b:oc * 4 + b + 1],
                                QA[:, oc * 2048 + b * 512: oc * 2048 + (b + 1) * 512],
                                AXL.X, AOT.add)
            P.on("vector", red)
            vv1 = evSv.inc()
            P.on("vector", lambda e: e.drain().then_inc(evSv.h, 1))
            P.on("tensor", lambda e, v=sv1: e.wait_ge(evSs.h, v))
            P.on("tensor", lambda e, v=vv1: e.wait_ge(evSv.h, v))
            mv2 = mmS.inc()
            for j in range(4):
                def trk(e, j=j):
                    e.transpose(pb[6][:, j * 128:(j + 1) * 128],
                                ktile[:, j * 128:(j + 1) * 128], T["ident"][:])
                P.on("tensor", trk)
                def trv(e, j=j):
                    m = e.transpose(pb[7][:, j * 128:(j + 1) * 128],
                                    vtile[:, j * 128:(j + 1) * 128], T["ident"][:])
                    if j == 3:
                        m.then_inc(mmS.h, 1)
                P.on("tensor", trv)
            def evkt(e, b=b, oc=oc, mv=mv2):
                e.wait_ge(mmS.h, mv)
                e.copy(KA.rearrange("p (c m) -> p c m", c=TC)
                       [:, 4 * b:4 * b + 4, oc * 128:(oc + 1) * 128],
                       pb[6].rearrange("p (j m) -> p j m", j=4))
                e.copy(vtile.rearrange("p (j m) -> p j m", j=4),
                       pb[7].rearrange("p (j m) -> p j m", j=4))
            P.on("scalar", evkt)
            sv2 = evSs.inc()
            P.on("scalar", lambda e: e.drain().then_inc(evSs.h, 1))
            grp_ev[g] = (sv2, vv1)
            def gv(e, b=b, oc=oc, sv=sv2, dVs=dVs):
                e.wait_ge(evSs.h, sv)
                e.dma_start(
                    out=T["l0a_v_d"].rearrange("(c p) m -> p c m", p=128)
                        [:, 4 * b:4 * b + 4, oc * 128:(oc + 1) * 128],
                    in_=vtile.rearrange("p (j m) -> p j m", j=4),
                ).then_inc(dVs.h, 16)
            P.on("gpsimd", gv)
            vtile_dma[g] = dVs.inc(16)
    bp1 = P.barrier(("sync", "tensor", "vector", "scalar", "gpsimd"))

    def cols_to_rows(cols_t, rows_t):
        for oc in range(DC):
            def tr(e, oc=oc, cols_t=cols_t):
                e.transpose(pb[6][0:4, 0:128], cols_t[:, oc * 4:(oc + 1) * 4],
                            T["ident"][:])
            P.on("tensor", tr)
            mv = mmS.inc()
            P.on("tensor", lambda e: e.drain().then_inc(mmS.h, 1))
            def cp(e, oc=oc, mv=mv, rows_t=rows_t):
                e.wait_ge(mmS.h, mv)
                e.copy(rows_t[0:4, oc * 128:(oc + 1) * 128], pb[6][0:4, 0:128])
            P.on("scalar", cp)
            sv = evSs.inc()
            P.on("scalar", lambda e: e.drain().then_inc(evSs.h, 1))
            P.on("tensor", lambda e, v=sv: e.wait_ge(evSs.h, v))
    cols_to_rows(T["ksum_cols"], ksum4)
    cols_to_rows(T["qsum_cols"], qsum4)
    br1 = P.barrier(("tensor", "scalar", "gpsimd"))
    dAR = P.dma_sem("dL0AR")
    def g_ar1(e, dAR=dAR):
        e.dma_start(out=T["l0a_ar1_d"][0:4, :], in_=ksum4[0:4, :]).then_inc(dAR.h, 16)
        e.dma_start(out=T["l0a_ar1_d"][4:8, :], in_=qsum4[0:4, :]).then_inc(dAR.h, 16)
    P.on("gpsimd", g_ar1)
    P.on("gpsimd", lambda e, n=dAR.inc(32): e.wait_ge(dAR.h, n))
    def g_ar1c(e):
        e.collective_compute(
            "AllReduce", AOT.add, replica_groups=[list(range(NCORES))],
            ins=[T["l0a_ar1_d"].opt()], outs=[T["l0a_ar1o_d"].opt()],
        ).then_inc(agS.h, 1)
    P.on("gpsimd", g_ar1c)
    P.on("gpsimd", lambda e, n=agS.inc(): e.wait_ge(agS.h, n))
    def g_r1l(e, dAR=dAR):
        e.dma_start(out=ksum4[0:4, :], in_=T["l0a_ar1o_d"][0:4, :]).then_inc(dAR.h, 16)
        e.dma_start(out=qsum4[0:4, :], in_=T["l0a_ar1o_d"][4:8, :]).then_inc(dAR.h, 16)
    P.on("gpsimd", g_r1l)
    P.on("gpsimd", lambda e, n=dAR.inc(32): e.wait_ge(dAR.h, n))
    vr1 = P.signal_S("gpsimd")

    P.wait_S("vector", vr1)
    P.on("vector", lambda e: e.tensor_scalar(ksum4[0:4, :], ksum4[0:4, :],
                                             EPS0, None, AOT.add))
    P.on("vector", lambda e: e.tensor_scalar(qsum4[0:4, :], qsum4[0:4, :],
                                             EPS0, None, AOT.add))
    P.guard("vector")
    def v_eps(e):
        e.tensor_reduce(eps416[0:4, 0:16],
                        ksum4[0:4, :].rearrange("p (h f) -> p h f", f=HD0),
                        AXL.X, AOT.add)
        e.tensor_reduce(eps416[0:4, 16:32],
                        qsum4[0:4, :].rearrange("p (h f) -> p h f", f=HD0),
                        AXL.X, AOT.add)
    P.on("vector", v_eps)
    P.guard("vector")
    P.on("vector", lambda e: e.tensor_scalar(eps416[0:4, 0:32], eps416[0:4, 0:32],
                                             EPS0, None, AOT.mult))
    veps = P.signal_S("vector")
    P.wait_S("gpsimd", veps)
    def g_epsq(e, dAR=dAR):
        e.dma_start(out=T["l0a_epsq_d"], in_=eps416[0:4, 16:32],
                    single_packet=True).then_inc(dAR.h, 16)
    P.on("gpsimd", g_epsq)
    P.on("gpsimd", lambda e, n=dAR.inc(16): e.wait_ge(dAR.h, n))
    P.wait_S("tensor", veps)
    def tr_eps(e):
        e.transpose(pb[6][0:16, 0:4], eps416[0:4, 0:16], T["ident"][0:4, 0:4])
    P.on("tensor", tr_eps)
    mv = mmS.inc()
    P.on("tensor", lambda e: e.drain().then_inc(mmS.h, 1))
    def cp_eps(e, mv=mv):
        e.wait_ge(mmS.h, mv)
        e.copy(epsT[0:16, 0:4], pb[6][0:16, 0:4])
    P.on("scalar", cp_eps)
    sv_eps = evSs.inc()
    P.on("scalar", lambda e: e.drain().then_inc(evSs.h, 1))

    def build_bd(src4):
        for dc in range(DC):
            def tr(e, dc=dc, src4=src4):
                e.transpose(pb[6][0:128, 0:4], src4[0:4, dc * 128:(dc + 1) * 128],
                            T["ident"][0:4, 0:4])
            P.on("tensor", tr)
            mv = mmS.inc()
            P.on("tensor", lambda e: e.drain().then_inc(mmS.h, 1))
            def cp(e, dc=dc, mv=mv):
                e.wait_ge(mmS.h, mv)
                for b_ in range(B):
                    col = (b_ * DC + dc) * 16
                    e.copy(bd[0:64, col + 2 * dc: col + 2 * dc + 1],
                           pb[6][0:64, b_:b_ + 1])
                    e.copy(bd[64:128, col + 2 * dc + 1: col + 2 * dc + 2],
                           pb[6][64:128, b_:b_ + 1])
            P.on("scalar", cp)
            sv = evSs.inc()
            P.on("scalar", lambda e: e.drain().then_inc(evSs.h, 1))
            P.on("tensor", lambda e, v=sv: e.wait_ge(evSs.h, v))
    P.on("vector", lambda e: e.memset(bd[:, :], 0.0))
    vbd = P.signal_S("vector")
    P.wait_S("tensor", vbd)
    P.on("tensor", lambda e, v=sv_eps: e.wait_ge(evSs.h, v))
    build_bd(ksum4)
    bbd = P.barrier(("tensor", "scalar", "vector"))

    def bd_dots(dst_rows, eps_col_off):
        for b_ in range(B):
            ps = pb[b_ % 2]
            mv = mmS.inc()
            for dc in range(DC):
                def mm(e, b_=b_, dc=dc, ps=ps):
                    m = e.matmul(ps[0:16, 0:512],
                                 bd[:, (b_ * DC + dc) * 16:(b_ * DC + dc + 1) * 16],
                                 QA[:, dc * 2048 + b_ * 512: dc * 2048 + (b_ + 1) * 512],
                                 start=(dc == 0), stop=(dc == DC - 1))
                    if dc == DC - 1:
                        m.then_inc(mmS.h, 1)
                P.on("tensor", mm)
            def ev(e, b_=b_, ps=ps, mv=mv, dst_rows=dst_rows, off=eps_col_off):
                e.wait_ge(mmS.h, mv)
                e.tensor_copy(dst_rows[0:16, b_ * 512:(b_ + 1) * 512],
                              ps[0:16, 0:512])
                e.tensor_scalar(dst_rows[0:16, b_ * 512:(b_ + 1) * 512],
                                dst_rows[0:16, b_ * 512:(b_ + 1) * 512],
                                epsT[0:16, off + b_:off + b_ + 1], None, AOT.add)
            P.on("vector", ev)
            vv = evSv.inc()
            P.on("vector", lambda e: e.drain().then_inc(evSv.h, 1))
            P.on("tensor", lambda e, v=vv: e.wait_ge(evSv.h, v))
    tap("ke", ksum4[0:4, :], veps)
    tap("qe", qsum4[0:4, :], veps)
    bd_dots(si_rows, 0)
    P.guard("vector")
    P.on("vector", lambda e: e.reciprocal(si_rows[0:16, :], si_rows[0:16, :]))
    vsi = P.signal_S("vector")
    tok_si_tap = tap("si", si_rows, vsi)

    qe_bc = scrD[:, 1024:2048]
    dPB = P.dma_sem("dL0PB")
    epsq_bc = T["epsq_bc"][:, 0:16]
    so = T["so_cols"]
    for b_ in range(B):
        if b_ >= 1:
            P.on("gpsimd", lambda e, v=vcons.n: e.wait_ge(vcons.h, v))
        def g_pb(e, b_=b_, dPB=dPB):
            e.dma_start(
                out=qe_bc.rearrange("p (a m) -> p a m", a=1),
                in_=T["l0a_ar1o_d"][4 + b_:5 + b_, :].rearrange("a m -> (a m)")
                    .partition_broadcast(128),
            ).then_inc(dPB.h, 16)
            e.dma_start(
                out=epsq_bc.rearrange("p (a m) -> p a m", a=1),
                in_=T["l0a_epsq_d"][b_:b_ + 1, :].rearrange("a m -> (a m)")
                    .partition_broadcast(128),
            ).then_inc(dPB.h, 16)
        P.on("gpsimd", g_pb)
        npb = dPB.inc(32)
        P.on("vector", lambda e, n=npb, dPB=dPB: e.wait_ge(dPB.h, n))
        P.on("vector", lambda e: e.tensor_scalar(qe_bc, qe_bc, EPS0, None, AOT.add))
        for i in range(4):
            tc = 4 * b_ + i
            def v1(e, tc=tc):
                e.tensor_tensor(scrD[:, 0:1024], tmj(KA, tc), qe_bc, AOT.mult)
                e.tensor_reduce(so[:, tc * 16:(tc + 1) * 16],
                                scrD[:, 0:1024].rearrange("p (h f) -> p h f", f=HD0),
                                AXL.X, AOT.add)
            P.on("vector", v1)
        P.guard("vector")
        for i in range(4):
            tc = 4 * b_ + i
            def v2(e, tc=tc):
                e.tensor_tensor(so[:, tc * 16:(tc + 1) * 16],
                                so[:, tc * 16:(tc + 1) * 16], epsq_bc, AOT.add)
            P.on("vector", v2)
        vcons.inc()
        P.on("vector", lambda e: e.drain().then_inc(vcons.h, 1))
    P.guard("vector")
    P.on("vector", lambda e: e.reciprocal(so[:, :], so[:, :]))
    vso = P.signal_S("vector")
    tap("so", so[:, :], vso)

    P.wait_S("tensor", vsi)
    ev_qsi = {}
    for i, (dc, b_) in enumerate([(d, b2) for d in range(DC) for b2 in range(B)]):
        ps = pb[2 + i % 2]
        if i >= 2:
            P.on("tensor", lambda e, v=ev_qsi[i - 2]: e.wait_ge(evSv.h, v))
        def mm(e, dc=dc, b_=b_, ps=ps):
            e.matmul(ps[:, 0:512], T["selpair"][0:16, dc * 128:(dc + 1) * 128],
                     si_rows[0:16, b_ * 512:(b_ + 1) * 512], start=True, stop=True)
        P.on("tensor", mm)
        mv = mmS.inc()
        P.on("tensor", lambda e: e.drain().then_inc(mmS.h, 1))
        def ev(e, dc=dc, b_=b_, ps=ps, mv=mv):
            e.wait_ge(mmS.h, mv)
            e.tensor_tensor(scrD[:, 0:512],
                            QA[:, dc * 2048 + b_ * 512: dc * 2048 + (b_ + 1) * 512],
                            ps[:, 0:512], AOT.mult)
            e.tensor_reduce(T["qsi_cols"][:, dc * 4 + b_:dc * 4 + b_ + 1],
                            scrD[:, 0:512], AXL.X, AOT.add)
        P.on("vector", ev)
        ev_qsi[i] = evSv.inc()
        P.on("vector", lambda e: e.drain().then_inc(evSv.h, 1))
    P.barrier(("tensor", "vector", "scalar"))

    P.wait_S("vector", vso)
    ev_kso = {}
    for tc in range(TC):
        par = tc % 2
        b4 = tc // 4
        if tc >= 2:
            P.on("vector", lambda e, v=grp_mm[("kso", tc - 2)]: e.wait_ge(mmS.h, v))
        def v1(e, tc=tc, par=par):
            e.tensor_tensor(
                scrD[:, par * 1024:(par + 1) * 1024]
                    .rearrange("p (h f) -> p h f", f=HD0),
                tmj(KA, tc).rearrange("p (h f) -> p h f", f=HD0),
                so[:, tc * 16:(tc + 1) * 16]
                    .rearrange("p (h x) -> p h x", x=1).broadcast_to([128, H0, HD0]),
                AOT.mult)
        P.on("vector", v1)
        vv = evSv.inc()
        P.on("vector", lambda e: e.drain().then_inc(evSv.h, 1))
        P.on("tensor", lambda e, v=vv: e.wait_ge(evSv.h, v))
        for half in range(2):
            def mm(e, tc=tc, par=par, b4=b4, half=half):
                e.matmul(pb[4 + half][0:4, 0:512],
                         T["sel4"][:, b4 * 4:(b4 + 1) * 4],
                         scrD[:, par * 1024 + half * 512: par * 1024 + (half + 1) * 512],
                         start=(tc == 0), stop=(tc == TC - 1))
            P.on("tensor", mm)
        grp_mm[("kso", tc)] = mmS.inc()
        P.on("tensor", lambda e: e.drain().then_inc(mmS.h, 1))
    mv_kso = grp_mm[("kso", TC - 1)]
    def ev_ks(e, mv=mv_kso):
        e.wait_ge(mmS.h, mv)
        e.copy(ksum4[0:4, 0:512], pb[4][0:4, 0:512])
        e.copy(ksum4[0:4, 512:1024], pb[5][0:4, 0:512])
    P.on("scalar", ev_ks)
    sv = evSs.inc()
    P.on("scalar", lambda e: e.drain().then_inc(evSs.h, 1))
    cols_to_rows(T["qsi_cols"], qsum4)
    br2 = P.barrier(("tensor", "scalar", "vector", "gpsimd"))

    def g_ar2(e, dAR=dAR):
        e.dma_start(out=T["l0a_ar2_d"][0:4, :], in_=qsum4[0:4, :]).then_inc(dAR.h, 16)
        e.dma_start(out=T["l0a_ar2_d"][4:8, :], in_=ksum4[0:4, :]).then_inc(dAR.h, 16)
    P.on("gpsimd", g_ar2)
    P.on("gpsimd", lambda e, n=dAR.inc(32): e.wait_ge(dAR.h, n))
    def g_ar2c(e):
        e.collective_compute(
            "AllReduce", AOT.add, replica_groups=[list(range(NCORES))],
            ins=[T["l0a_ar2_d"].opt()], outs=[T["l0a_ar2o_d"].opt()],
        ).then_inc(agS.h, 1)
    P.on("gpsimd", g_ar2c)
    P.on("gpsimd", lambda e, n=agS.inc(): e.wait_ge(agS.h, n))
    def g_r2l(e, dAR=dAR):
        e.dma_start(out=ksum4[0:4, :], in_=T["l0a_ar2o_d"][4:8, :]).then_inc(dAR.h, 16)
        e.dma_start(out=qsum4[0:4, :], in_=T["l0a_ar2o_d"][0:4, :]).then_inc(dAR.h, 16)
    P.on("gpsimd", g_r2l)
    P.on("gpsimd", lambda e, n=dAR.inc(32): e.wait_ge(dAR.h, n))
    vr2 = P.signal_S("gpsimd")

    P.wait_S("vector", vr2)
    P.on("vector", lambda e: e.tensor_scalar(ksum4[0:4, :], ksum4[0:4, :],
                                             EPS0, None, AOT.add))
    P.on("vector", lambda e: e.tensor_scalar(qsum4[0:4, :], qsum4[0:4, :],
                                             EPS0, None, AOT.add))
    P.guard("vector")
    def v_eps2(e):
        e.tensor_reduce(eps416[0:4, 32:48],
                        ksum4[0:4, :].rearrange("p (h f) -> p h f", f=HD0),
                        AXL.X, AOT.add)
        e.tensor_reduce(eps416[0:4, 48:64],
                        qsum4[0:4, :].rearrange("p (h f) -> p h f", f=HD0),
                        AXL.X, AOT.add)
    P.on("vector", v_eps2)
    P.guard("vector")
    P.on("vector", lambda e: e.tensor_scalar(eps416[0:4, 32:64], eps416[0:4, 32:64],
                                             EPS0, None, AOT.mult))
    veps2 = P.signal_S("vector")
    P.wait_S("gpsimd", veps2)
    def g_epsq2(e, dAR=dAR):
        e.dma_start(out=T["l0a_epsq_d"], in_=eps416[0:4, 48:64],
                    single_packet=True).then_inc(dAR.h, 16)
    P.on("gpsimd", g_epsq2)
    P.on("gpsimd", lambda e, n=dAR.inc(16): e.wait_ge(dAR.h, n))
    P.wait_S("tensor", veps2)
    def tr_eps2(e):
        e.transpose(pb[6][0:16, 0:4], eps416[0:4, 32:48], T["ident"][0:4, 0:4])
    P.on("tensor", tr_eps2)
    mv = mmS.inc()
    P.on("tensor", lambda e: e.drain().then_inc(mmS.h, 1))
    def cp_eps2(e, mv=mv):
        e.wait_ge(mmS.h, mv)
        e.copy(epsT[0:16, 4:8], pb[6][0:16, 0:4])
    P.on("scalar", cp_eps2)
    P.on("scalar", lambda e: e.drain().then_inc(evSs.h, 1))
    evSs.inc()
    build_bd(ksum4)
    P.barrier(("tensor", "scalar", "vector"))
    bd_dots(sa_rows, 4)
    bsa = P.barrier(("vector", "tensor"))
    P.wait_S("scalar", bsa)
    P.on("scalar", lambda e: e.activation(sa_rows[0:16, :], sa_rows[0:16, :],
                                          AFT.Sigmoid))
    vsa = P.signal_S("scalar")
    tap("sa", sa_rows, vsa)

    ec = T["e_cols"]
    for b_ in range(B):
        if b_ >= 1:
            P.on("gpsimd", lambda e, v=vcons.n: e.wait_ge(vcons.h, v))
        def g_pb2(e, b_=b_, dPB=dPB):
            e.dma_start(
                out=qe_bc.rearrange("p (a m) -> p a m", a=1),
                in_=T["l0a_ar2o_d"][b_:b_ + 1, :].rearrange("a m -> (a m)")
                    .partition_broadcast(128),
            ).then_inc(dPB.h, 16)
            e.dma_start(
                out=epsq_bc.rearrange("p (a m) -> p a m", a=1),
                in_=T["l0a_epsq_d"][b_:b_ + 1, :].rearrange("a m -> (a m)")
                    .partition_broadcast(128),
            ).then_inc(dPB.h, 16)
        P.on("gpsimd", g_pb2)
        npb = dPB.inc(32)
        P.on("vector", lambda e, n=npb, dPB=dPB: e.wait_ge(dPB.h, n))
        P.on("vector", lambda e: e.tensor_scalar(qe_bc, qe_bc, EPS0, None, AOT.add))
        for i in range(4):
            tc = 4 * b_ + i
            def v1(e, tc=tc):
                e.tensor_tensor(scrD[:, 0:1024], tmj(KA, tc), qe_bc, AOT.mult)
                e.tensor_reduce(ec[:, tc * 16:(tc + 1) * 16],
                                scrD[:, 0:1024].rearrange("p (h f) -> p h f", f=HD0),
                                AXL.X, AOT.add)
            P.on("vector", v1)
        P.guard("vector")
        for i in range(4):
            tc = 4 * b_ + i
            def v2(e, tc=tc):
                e.tensor_tensor(ec[:, tc * 16:(tc + 1) * 16],
                                ec[:, tc * 16:(tc + 1) * 16], epsq_bc, AOT.add)
            P.on("vector", v2)
        vcons.inc()
        P.on("vector", lambda e: e.drain().then_inc(vcons.h, 1))
    P.guard("vector")
    P.on("vector", lambda e: e.tensor_scalar(ec[:, :], ec[:, :],
                                             1.0, -1.0, AOT.min, AOT.max))
    vclip = P.signal_S("vector")
    P.wait_S("scalar", vclip)
    P.on("scalar", lambda e: e.activation(ec[:, :], ec[:, :], AFT.Exp))
    vexp = P.signal_S("scalar")
    tap("e", ec[:, :], vexp)

    P.wait_S("tensor", vexp)
    for tc in range(TC):
        def mm(e, tc=tc):
            e.matmul(pb[3][0:4, 0:16], T["sel4"][:, (tc // 4) * 4:(tc // 4 + 1) * 4],
                     ec[:, tc * 16:(tc + 1) * 16],
                     start=(tc == 0), stop=(tc == TC - 1))
        P.on("tensor", mm)
    mvz = mmS.inc()
    P.on("tensor", lambda e: e.drain().then_inc(mmS.h, 1))
    def ev_z(e, mv=mvz):
        e.wait_ge(mmS.h, mv)
        e.copy(T["z4"][0:4, 0:16], pb[3][0:4, 0:16])
    P.on("scalar", ev_z)
    P.on("scalar", lambda e: e.drain().then_inc(evSs.h, 1))
    svz = evSs.n + 0
    evSs.inc()
    svz = evSs.n
    P.on("vector", lambda e: e.memset(zrow, 0.0))
    vz2 = P.signal_S("vector")
    P.wait_S("gpsimd", vz2)
    P.on("gpsimd", lambda e, v=svz: e.wait_ge(evSs.h, v))
    def g_z(e, dAR=dAR):
        e.dma_start(out=T["l0a_kv_d"][128:129, 0:64]
                    .rearrange("a (b h) -> (a b) h", b=4),
                    in_=T["z4"][0:4, 0:16], single_packet=True).then_inc(dAR.h, 16)
        e.dma_start(out=T["l0a_kv_d"][128:129, 64:1056], in_=zrow,
                    single_packet=True).then_inc(dAR.h, 16)
        e.dma_start(out=T["l0a_kv_d"][128:129, 1056:2048], in_=zrow,
                    single_packet=True).then_inc(dAR.h, 16)
    P.on("gpsimd", g_z)
    P.on("gpsimd", lambda e, n=dAR.inc(48): e.wait_ge(dAR.h, n))

    P.wait_S("vector", vexp)
    dVr = P.dma_sem("dL0Vr")
    kv_ev = {}
    for b_ in range(B):
        if b_ >= 1:
            P.on("sync", lambda e, v=grp_mm[("kv", b_ - 1)]: e.wait_ge(mmS.h, v))
        def fv(e, b_=b_, dVr=dVr):
            e.dma_start(
                out=VA.rearrange("p (c m) -> p c m", c=4),
                in_=T["l0a_v_d"][4 * b_ * 128:(4 * b_ + 4) * 128, :]
                    .rearrange("(c p) m -> p c m", p=128),
            ).then_inc(dVr.h, 16)
        P.on("sync", fv)
        nvr = dVr.inc(16)
        P.on("vector", lambda e, n=nvr, dVr=dVr: e.wait_ge(dVr.h, n))
        for i in range(4):
            tc = 4 * b_ + i
            def v1(e, tc=tc, i=i):
                e.tensor_tensor(
                    VA[:, i * 1024:(i + 1) * 1024]
                        .rearrange("p (h f) -> p h f", f=HD0),
                    VA[:, i * 1024:(i + 1) * 1024]
                        .rearrange("p (h f) -> p h f", f=HD0),
                    ec[:, tc * 16:(tc + 1) * 16]
                        .rearrange("p (h x) -> p h x", x=1).broadcast_to([128, H0, HD0]),
                    AOT.mult)
            P.on("vector", v1)
        vv = evSv.inc()
        P.on("vector", lambda e: e.drain().then_inc(evSv.h, 1))
        P.on("tensor", lambda e, v=vv: e.wait_ge(evSv.h, v))
        if b_ >= 2:
            P.on("tensor", lambda e, v=kv_ev[b_ - 2]: e.wait_ge(evSs.h, v))
        grp_mm[("kv", b_)] = mmS.inc()
        for hp in range(DC):
            ps = pb[4 + 2 * (b_ % 2) + hp // 4]
            for j in range(4):
                def mm(e, b_=b_, hp=hp, j=j, ps=ps, fin=(hp == DC - 1)):
                    m = e.matmul(ps[:, (hp % 4) * 128:(hp % 4 + 1) * 128],
                                 tmj(KA, 4 * b_ + j)[:, hp * 128:(hp + 1) * 128],
                                 VA[:, j * 1024 + hp * 128: j * 1024 + (hp + 1) * 128],
                                 start=(j == 0), stop=(j == 3))
                    if fin and j == 3:
                        m.then_inc(mmS.h, 1)
                P.on("tensor", mm)
        def ev(e, b_=b_, mv=grp_mm[("kv", b_)]):
            e.wait_ge(mmS.h, mv)
            for hp in range(DC):
                ps = pb[4 + 2 * (b_ % 2) + hp // 4]
                col = (b_ * DC + hp) * 64
                e.copy(scrD[0:64, col:col + 64],
                       ps[0:64, (hp % 4) * 128:(hp % 4) * 128 + 64])
                e.copy(scrD[64:128, col:col + 64],
                       ps[64:128, (hp % 4) * 128 + 64:(hp % 4 + 1) * 128])
        P.on("scalar", ev)
        kv_ev[b_] = evSs.inc()
        P.on("scalar", lambda e: e.drain().then_inc(evSs.h, 1))
    bkv = P.barrier(("tensor", "scalar", "vector", "sync", "gpsimd"))

    def g_kv(e, dAR=dAR):
        e.dma_start(out=T["l0a_kv_d"][0:128, :], in_=scrD[:, 0:2048]).then_inc(dAR.h, 16)
    P.on("gpsimd", g_kv)
    P.on("gpsimd", lambda e, n=dAR.inc(16): e.wait_ge(dAR.h, n))
    def g_ar3(e):
        e.collective_compute(
            "AllReduce", AOT.add, replica_groups=[list(range(NCORES))],
            ins=[T["l0a_kv_d"].opt()], outs=[T["l0a_kvo_d"].opt()],
        ).then_inc(agS.h, 1)
    P.on("gpsimd", g_ar3)
    P.on("gpsimd", lambda e, n=agS.inc(): e.wait_ge(agS.h, n))
    P.on("vector", lambda e: e.memset(VA[:, :], 0.0))
    vmz = P.signal_S("vector")
    P.wait_S("gpsimd", vmz)
    def g_kvl(e, dAR=dAR):
        e.dma_start(
            out=VA[0:64, :].rearrange("p (t u) -> p t u", t=32)[:, :, 0:64],
            in_=T["l0a_kvo_d"][0:64, :].rearrange("p (t u) -> p t u", t=32),
        ).then_inc(dAR.h, 16)
        e.dma_start(
            out=VA[64:128, :].rearrange("p (t u) -> p t u", t=32)[:, :, 64:128],
            in_=T["l0a_kvo_d"][64:128, :].rearrange("p (t u) -> p t u", t=32),
        ).then_inc(dAR.h, 16)
        e.dma_start(
            out=T["z4"][0:4, 0:16],
            in_=T["l0a_kvo_d"][128:129, 0:64].rearrange("a (b h) -> (a b) h", b=4),
            single_packet=True).then_inc(dAR.h, 16)
    P.on("gpsimd", g_kvl)
    P.on("gpsimd", lambda e, n=dAR.inc(48): e.wait_ge(dAR.h, n))
    vkv = P.signal_S("gpsimd")
    P.wait_S("vector", vkv)
    P.on("vector", lambda e: e.reciprocal(T["z4"][0:4, 0:16], T["z4"][0:4, 0:16]))
    P.guard("vector")
    P.on("vector", lambda e: e.tensor_scalar(T["z4"][0:4, 0:16], T["z4"][0:4, 0:16],
                                             float(S), None, AOT.mult))
    vz3 = P.signal_S("vector")
    P.wait_S("tensor", vz3)
    def tr_z(e):
        e.transpose(pb[6][0:16, 0:4], T["z4"][0:4, 0:16], T["ident"][0:4, 0:4])
    P.on("tensor", tr_z)
    mvz2 = mmS.inc()
    P.on("tensor", lambda e: e.drain().then_inc(mmS.h, 1))
    def cp_z(e, mv=mvz2):
        e.wait_ge(mmS.h, mv)
        e.copy(epsT[0:16, 8:12], pb[6][0:16, 0:4])
    P.on("scalar", cp_z)
    P.on("scalar", lambda e: e.drain().then_inc(evSs.h, 1))
    evSs.inc()
    bz = P.barrier(("vector", "scalar", "tensor"))
    tok_kv_tap = tap("kv", VA[:, :], bz)
    tap("z", T["z4"][0:4, 0:16], bz)

    P.wait_S("vector", vsa)
    if tok_si_tap is not None:
        P.wait_S("vector", tok_si_tap)
    P.on("vector", lambda e: e.tensor_tensor(si_rows[0:16, :], si_rows[0:16, :],
                                             sa_rows[0:16, :], AOT.mult))
    P.guard("vector")
    for b_ in range(B):
        def vc(e, b_=b_):
            e.tensor_scalar(si_rows[0:16, b_ * 512:(b_ + 1) * 512],
                            si_rows[0:16, b_ * 512:(b_ + 1) * 512],
                            epsT[0:16, 8 + b_:9 + b_], None, AOT.mult)
        P.on("vector", vc)
    vcr = P.signal_S("vector")
    P.wait_S("tensor", vcr)
    o_ev = {}
    for i, (b_, dc) in enumerate([(b2, d) for b2 in range(B) for d in range(DC)]):
        ps_o = pb[i % 4]
        ps_c = pb[4 + i % 4]
        if i >= 4:
            P.on("tensor", lambda e, v=o_ev[i - 4]: e.wait_ge(evSv.h, v))
        def mmc(e, b_=b_, dc=dc, ps_c=ps_c):
            e.matmul(ps_c[:, 0:512], T["selpair"][0:16, dc * 128:(dc + 1) * 128],
                     si_rows[0:16, b_ * 512:(b_ + 1) * 512], start=True, stop=True)
        P.on("tensor", mmc)
        mv = mmS.inc()
        def mmo(e, b_=b_, dc=dc, ps_o=ps_o):
            e.matmul(ps_o[:, 0:512], VA[:, (b_ * DC + dc) * 128:(b_ * DC + dc + 1) * 128],
                     QA[:, dc * 2048 + b_ * 512: dc * 2048 + (b_ + 1) * 512],
                     start=True, stop=True).then_inc(mmS.h, 1)
        P.on("tensor", mmo)
        cstage = scrD[:, (i % 2) * 512:(i % 2 + 1) * 512]
        def evc(e, ps_c=ps_c, mv=mv, cstage=cstage, i=i):
            e.wait_ge(mmS.h, mv)
            if i >= 2:
                e.wait_ge(evSv.h, o_ev[i - 2])
            e.copy(cstage, ps_c[:, 0:512])
        P.on("scalar", evc)
        sv = evSs.inc()
        P.on("scalar", lambda e: e.drain().then_inc(evSs.h, 1))
        def ev(e, b_=b_, dc=dc, ps_o=ps_o, mv=mv, sv=sv, cstage=cstage):
            e.wait_ge(mmS.h, mv)
            e.wait_ge(evSs.h, sv)
            e.tensor_tensor(KA[:, dc * 2048 + b_ * 512: dc * 2048 + (b_ + 1) * 512],
                            ps_o[:, 0:512], cstage, AOT.mult)
        P.on("vector", ev)
        o_ev[i] = evSv.inc()
        P.on("vector", lambda e: e.drain().then_inc(evSv.h, 1))
    bp4 = P.barrier(("tensor", "vector", "scalar", "sync", "gpsimd"))
    tok_o_tap = tap("o", KA[:, 0:2048], bp4)
    tok_c_tap = tap("c", si_rows, bp4)
    if tok_kv_tap is not None or tok_o_tap is not None:
        for tk in (tok_kv_tap, tok_o_tap, tok_c_tap):
            if tk is not None:
                P.wait_S("sync", tk)

    dWo = P.dma_sem("dL0Wo")
    if "l0a_wo_slots" in T:
        woap = T["l0a_wo_slots"]
        boap = T["l0a_bo_slot"]
    else:
        woap = [QA[:, ofc * 1024:(ofc + 1) * 1024] for ofc in range(DC)]
        boap = QA[0:1, 8192:9216]
    x1res = T.get("l0a_x1_resident")
    def fwo(e, dWo=dWo):
        for ofc in range(DC):
            e.dma_start(out=woap[ofc],
                        in_=T["l0a_wo_d"][ofc]).then_inc(dWo.h, 16)
        e.dma_start(out=boap, in_=T["l0a_bo_d"],
                    single_packet=True).then_inc(dWo.h, 16)
    P.on("sync", fwo)
    nwo = dWo.inc(16 * 9)
    P.on("tensor", lambda e, n=nwo, dWo=dWo: e.wait_ge(dWo.h, n))
    dXs = P.dma_sem("dL0Xs")
    dOut = P.dma_sem("dL0Out")
    lnd = P.aux_sem("l0a_lnd")
    x_ev = {}
    for tc in range(TC):
        par = tc % 2
        b_, lb = tc // 4, tc % 4
        if tc >= 2:
            P.on("sync", lambda e, v=x_ev[tc - 2]: e.wait_ge(evSv.h, v))
        def fx(e, tc=tc, par=par, dXs=dXs):
            e.dma_start(out=VA[:, par * 1024:(par + 1) * 1024],
                        in_=x_dram[tc * 128:(tc + 1) * 128, :]).then_inc(dXs.h, 16)
        P.on("sync", fx)
        nx = dXs.inc(16)
        if tc >= 2:
            P.on("tensor", lambda e, v=o_ev[("p5", tc - 2)]: e.wait_ge(evSv.h, v))
        for dh in range(2):
            ps = pb[par * 2 + dh]
            for ofc in range(DC):
                def mm(e, ps=ps, ofc=ofc, b_=b_, lb=lb, dh=dh):
                    e.matmul(ps[:, 0:512],
                             KA[:, ofc * 2048 + b_ * 512 + lb * 128:
                                ofc * 2048 + b_ * 512 + (lb + 1) * 128],
                             woap[ofc][:, dh * 512:(dh + 1) * 512],
                             start=(ofc == 0), stop=False)
                P.on("tensor", mm)
            def mmb(e, ps=ps, dh=dh):
                m = e.matmul(ps[:, 0:512], T["ones_row"][0:1, 0:128],
                             boap[0:1, dh * 512:(dh + 1) * 512],
                             start=False, stop=True)
                if dh == 1:
                    m.then_inc(mmS.h, 1)
            P.on("tensor", mmb)
        mv = mmS.inc()
        preln = VA[:, 2048 + par * 1024:2048 + (par + 1) * 1024]
        def ev(e, tc=tc, par=par, mv=mv, nx=nx, preln=preln, dXs=dXs,
               res=(x1res is not None)):
            e.wait_ge(mmS.h, mv)
            e.wait_ge(dXs.h, nx)
            if tc >= 2 and not res:
                e.wait_ge(dOut.h, 16 * (tc - 1))
            for dh in range(2):
                e.tensor_tensor(preln[:, dh * 512:(dh + 1) * 512],
                                pb[par * 2 + dh][:, 0:512],
                                VA[:, par * 1024 + dh * 512: par * 1024 + (dh + 1) * 512],
                                AOT.add)
        P.on("vector", ev)
        x_ev[tc] = o_ev[("p5", tc)] = evSv.inc()
        P.on("vector", lambda e: e.drain().then_inc(evSv.h, 1))
        if x1res is not None:
            def vout(e, tc=tc, preln=preln):
                e.tensor_copy(x1res[:, tc * 1024:(tc + 1) * 1024], preln)
            P.on("vector", vout)
        else:
            emit_ln(P, preln, preln, scrD[:, 0:1024], T["mean_s"],
                    scrD[:, 1024:2048], nchunk=1, wd=1024)
            nl = lnd.inc()
            P.on("vector", lambda e: e.drain().then_inc(lnd.h, 1))
            def gout(e, tc=tc, preln=preln, nl=nl, dOut=dOut):
                e.wait_ge(lnd.h, nl)
                e.dma_start(out=out_dram[tc * 128:(tc + 1) * 128, :],
                            in_=preln).then_inc(dOut.h, 16)
            P.on("gpsimd", gout)
            dOut.inc(16)
    if x1res is None:
        P.on("gpsimd", lambda e, n=dOut.n: e.wait_ge(dOut.h, n))
        P.signal_S("gpsimd")
    else:
        emit_ln(P, x1res, x1res, scrD[:, 0:1024], T["mean_s"],
                scrD[:, 1024:2048], nchunk=TC, wd=1024)
    P.barrier(("gpsimd", "vector", "scalar", "tensor", "sync"))


NB5 = 5
BW = NBIN


def _spec_slices(t, b_t=512):
    return [(t[:, 0:4 * b_t], True), (t[0:1, 4 * b_t:5 * b_t], False)]


def emit_l1attn(P, T, x2_dram, out_dram, batches=tuple(range(B))):
    nc = P.nc
    wa = T["warena"]
    fre, fim = T["fre"], T["fim"]
    xbuf = T["xbuf"]
    x2T = T["x2T"]
    qkv = T["qkv"]
    fqre, fqim = T["fq_re"], T["fq_im"]
    fkre, fkim = T["fk_re"], T["fk_im"]
    pre_, pim_ = T["pv_re"], T["pv_im"]
    spare = T["scr_c"]
    spare2 = T["scr_spare2"]
    x3b = T["x3b"]
    pb = T["psum"]
    dW = P.dma_sem("dW")
    dX = P.dma_sem("dX2")
    mmS = P.aux_sem("l1a_mm")
    evS = P.aux_sem("l1a_ev")
    agS = P.aux_sem("l1a_ag")

    fnyq = T["fnyq"]
    def fload(e):
        e.dma_start(out=fre[:].rearrange("p (c m) -> p c m", c=DC),
                    in_=T["fre_d"][:, 0:512].rearrange("(c p) m -> p c m", p=128)).then_inc(dW.h, 16)
        e.dma_start(out=fim[:].rearrange("p (c m) -> p c m", c=DC),
                    in_=T["fim_d"][:, 0:512].rearrange("(c p) m -> p c m", p=128)).then_inc(dW.h, 16)
        e.dma_start(out=fnyq[:, 0:8],
                    in_=T["fre_d"][:, 512:513].rearrange("(c p) m -> p (c m)", p=128),
                    allow_slow_non_contiguous=True).then_inc(dW.h, 16)
        e.dma_start(out=fnyq[:, 8:16],
                    in_=T["fim_d"][:, 512:513].rearrange("(c p) m -> p (c m)", p=128),
                    allow_slow_non_contiguous=True).then_inc(dW.h, 16)
    P.on("sync", fload)
    dW.inc(64)

    def load_w(which_d):
        def f(e, which_d=which_d):
            e.dma_start(out=wa[:, :DC * D].rearrange("p (c m) -> p c m", c=DC),
                        in_=which_d.rearrange("(c p) m -> p c m", p=128)).then_inc(dW.h, 16)
        P.on("sync", f)
        return dW.inc(16)

    def load_ir():
        irre = T["ir_re"]
        irim = T["ir_im"]
        def f(e):
            e.dma_start(out=irre[:, 0:4 * D].rearrange("p (c m) -> p c m", c=4),
                        in_=T["irre_d"][0:512, :].rearrange("(c p) m -> p c m", p=128)).then_inc(dW.h, 16)
            e.dma_start(out=irre[0:1, 4 * D:5 * D], in_=T["irre_d"][512:513, :], single_packet=True).then_inc(dW.h, 16)
            e.dma_start(out=irim[:, 0:4 * D].rearrange("p (c m) -> p c m", c=4),
                        in_=T["irim_d"][0:512, :].rearrange("(c p) m -> p c m", p=128)).then_inc(dW.h, 16)
            e.dma_start(out=irim[0:1, 4 * D:5 * D], in_=T["irim_d"][512:513, :], single_packet=True).then_inc(dW.h, 16)
        P.on("sync", f)
        return dW.inc(64)

    def proj_to(dst, b):
        for ec in range(DC):
            ps = pb[ec % 4]
            if ec >= 4:
                P.on("tensor", lambda e, v=evS.n - 3: e.wait_ge(evS.h, v))
            mv = mmS.inc()
            for dc in range(DC):
                def mm(e, ec=ec, dc=dc, ps=ps):
                    m = e.matmul(ps[:, 0:512],
                                 wa[:, dc * D + ec * 128: dc * D + (ec + 1) * 128],
                                 x2T[:, dc * 512:(dc + 1) * 512],
                                 start=(dc == 0), stop=(dc == DC - 1))
                    if dc == DC - 1:
                        m.then_inc(mmS.h, 1)
                P.on("tensor", mm)

            def ev(e, ec=ec, ps=ps, mv=mv, dst=dst):
                e.wait_ge(mmS.h, mv)
                e.copy(dst[:, ec * 512:(ec + 1) * 512], ps[:, 0:512])
            P.on("scalar", ev)
            evS.inc()
            P.on("scalar", lambda e: e.drain().then_inc(evS.h, 1))

    def dft_to(dre, dim_, src):
        for mi, (mat, dst) in enumerate(((fre, dre), (fim, dim_))):
            for bc in range(NB5):
                mw = 128 if bc < 4 else 1
                ps = pb[(4 + bc) % 8]
                if True:
                    P.on("tensor", lambda e, v=max(0, evS.n - 3): e.wait_ge(evS.h, v))
                mv = mmS.inc()
                for ec in range(DC):
                    def mm(e, bc=bc, ec=ec, mw=mw, ps=ps, mat=mat, mi=mi):
                        lhsT = (mat[:, ec * 512 + bc * 128: ec * 512 + bc * 128 + mw]
                                if bc < 4 else fnyq[:, mi * 8 + ec: mi * 8 + ec + 1])
                        m = e.matmul(ps[0:mw, 0:512], lhsT,
                                     src[:, ec * 512:(ec + 1) * 512],
                                     start=(ec == 0), stop=(ec == DC - 1))
                        if ec == DC - 1:
                            m.then_inc(mmS.h, 1)
                    P.on("tensor", mm)

                def ev(e, bc=bc, mw=mw, ps=ps, mv=mv, dst=dst):
                    e.wait_ge(mmS.h, mv)
                    e.copy(dst[0:mw, bc * 512:(bc + 1) * 512], ps[0:mw, 0:512])
                P.on("scalar", ev)
                evS.inc()
                P.on("scalar", lambda e: e.drain().then_inc(evS.h, 1))

    def normalize(re_t, im_t):
        nsem = P.aux_sem("l1a_nrm")
        for sl_re, sl_im, sl_sp, sl_sp2 in (
            (re_t[:, 0:2048], im_t[:, 0:2048], spare[:, 0:2048], spare2[:, 0:2048]),
            (re_t[0:1, 2048:2560], im_t[0:1, 2048:2560], spare[0:1, 0:512], spare2[0:1, 0:512]),
        ):
            def v1(e, a=sl_re, b=sl_im, s=sl_sp, s2=sl_sp2):
                e.tensor_tensor(s, a, a, AOT.mult)
                e.tensor_tensor(s2, b, b, AOT.mult)
                e.tensor_tensor(s, s, s2, AOT.add)
            P.on("vector", v1)
            t1 = nsem.inc()
            P.on("vector", lambda e: e.drain().then_inc(nsem.h, 1))

            def s1(e, s=sl_sp, t1=t1):
                e.wait_ge(nsem.h, t1)
                e.activation(s, s, AFT.Sqrt)
            P.on("scalar", s1)
            t2 = nsem.inc()
            P.on("scalar", lambda e: e.drain().then_inc(nsem.h, 1))

            def v2(e, a=sl_re, b=sl_im, s=sl_sp, t2=t2):
                e.wait_ge(nsem.h, t2)
                e.tensor_scalar(s, s, 1e-8, None, AOT.add)
                e.reciprocal(s, s)
                e.tensor_tensor(a, a, s, AOT.mult)
                e.tensor_tensor(b, b, s, AOT.mult)
            P.on("vector", v2)

    nW = None
    first_b = batches[0]
    for b in batches:
        base = P.barrier(("sync", "tensor", "vector", "scalar", "gpsimd"))
        if b == first_b:
            P.on("vector", lambda e: e.memset(T["totals"][:, 0:10], 0.0))
            P.signal_S("vector")
        mm_base = mmS.n
        for i in range(4):
            tc = 4 * b + i
            par = i % 2
            dXp = P.dma_sem(f"dX2p{par}")
            if i >= 2:
                P.on("sync", lambda e, v=mm_base + 8 * (i - 1): e.wait_ge(mmS.h, v))
            def fx(e, tc=tc, par=par, dXp=dXp):
                e.dma_start(out=xbuf[:, par * D:(par + 1) * D],
                            in_=x2_dram[tc * 128:(tc + 1) * 128, :]).then_inc(dXp.h, 16)
            P.on("sync", fx)
            nx = dXp.inc(16)
            P.on("tensor", lambda e, n=nx, dXp=dXp: e.wait_ge(dXp.h, n))
            for dc in range(DC):
                ps = pb[dc % 2]
                if True:
                    P.on("tensor", lambda e, v=max(0, evS.n - 1): e.wait_ge(evS.h, v))
                mv = mmS.inc()
                def tr(e, dc=dc, par=par, ps=ps):
                    e.transpose(ps[:, 0:128], xbuf[:, par * D + dc * 128: par * D + (dc + 1) * 128],
                                T["ident"][:]).then_inc(mmS.h, 1)
                P.on("tensor", tr)

                def ev(e, dc=dc, i=i, ps=ps, mv=mv):
                    e.wait_ge(mmS.h, mv)
                    e.copy(x2T[:, dc * 512 + i * 128: dc * 512 + (i + 1) * 128], ps[:, 0:128])
                P.on("scalar", ev)
                evS.inc()
                P.on("scalar", lambda e: e.drain().then_inc(evS.h, 1))
        nW = load_w(T["wqT_d"])
        P.barrier(("sync", "tensor", "vector", "scalar"))
        P.on("tensor", lambda e, n=nW: e.wait_ge(dW.h, n))
        proj_to(qkv, b)
        P.barrier(("tensor", "scalar", "sync"))
        nW = load_w(T["wkT_d"])
        dft_to(fqre, fqim, qkv)
        P.barrier(("tensor", "scalar", "sync", "vector"))
        normalize(fqre, fqim)
        P.on("tensor", lambda e, n=nW: e.wait_ge(dW.h, n))
        proj_to(qkv, b)
        P.barrier(("tensor", "scalar", "sync"))
        nW = load_w(T["wvT_d"])
        dft_to(fkre, fkim, qkv)
        P.barrier(("tensor", "scalar", "sync", "vector"))
        normalize(fkre, fkim)
        P.on("tensor", lambda e, n=nW: e.wait_ge(dW.h, n))
        proj_to(qkv, b)
        P.barrier(("tensor", "scalar", "sync"))
        nIR = load_ir()
        dft_to(pre_, pim_, qkv)
        bv = P.barrier(("tensor", "scalar", "vector"))

        spB, spB2 = qkv[:, 0:2048], qkv[:, 2048:4096]
        for a_re, a_im, v_re, v_im, s1_, s2_ in (
            (fkre[:, 0:2048], fkim[:, 0:2048], pre_[:, 0:2048], pim_[:, 0:2048],
             spB[:, 0:2048], spB2[:, 0:2048]),
            (fkre[0:1, 2048:2560], fkim[0:1, 2048:2560], pre_[0:1, 2048:2560],
             pim_[0:1, 2048:2560], spB[0:1, 0:512], spB2[0:1, 0:512]),
        ):
            def vP(e, a=a_re, ai=a_im, vr=v_re, vi=v_im, s1=s1_, s2=s2_):
                e.tensor_tensor(s1, ai, vr, AOT.mult)
                e.tensor_tensor(s2, ai, vi, AOT.mult)
                e.tensor_tensor(vr, a, vr, AOT.mult)
                e.tensor_tensor(vr, vr, s2, AOT.subtract)
                e.tensor_tensor(vi, a, vi, AOT.mult)
                e.tensor_tensor(vi, vi, s1, AOT.add)
            P.on("vector", vP)

        for t_ in (pre_, pim_):
            for bc in range(NB5):
                mw = 128 if bc < 4 else 1
                def vs(e, t_=t_, bc=bc, mw=mw):
                    e.tensor_tensor_scan(t_[0:mw, bc * 512:(bc + 1) * 512],
                                         t_[0:mw, bc * 512:(bc + 1) * 512],
                                         t_[0:mw, bc * 512:(bc + 1) * 512],
                                         0.0, AOT.add, AOT.bypass)
                P.on("vector", vs)
        P.guard("vector")
        for r, t_ in enumerate((pre_, pim_)):
            for bc in range(NB5):
                mw = 128 if bc < 4 else 1
                def vt(e, r=r, bc=bc, mw=mw, t_=t_):
                    e.tensor_copy(T["totals"][0:mw, r * NB5 + bc: r * NB5 + bc + 1],
                                  t_[0:mw, bc * 512 + 511: bc * 512 + 512])
                P.on("vector", vt)
        vtot = P.signal_S("vector")

        P.wait_S("gpsimd", vtot)
        dG = P.dma_sem("dAg")
        def g1(e):
            e.dma_start(out=T["ag_in_d"], in_=T["totals"][:, 0:10]).then_inc(dG.h, 16)
        P.on("gpsimd", g1)
        n1 = dG.inc(16)
        P.on("gpsimd", lambda e, n=n1: e.wait_ge(dG.h, n))

        def g2(e):
            e.collective_compute(
                "AllGather", AOT.bypass, replica_groups=[list(range(NCORES))],
                ins=[T["ag_in_d"].opt()], outs=[T["ag_out_d"].opt()],
            ).then_inc(agS.h, 1)
        P.on("gpsimd", g2)
        na = agS.inc()
        P.on("gpsimd", lambda e, n=na: e.wait_ge(agS.h, n))
        for kk in range(NCORES):
            def g3(e, kk=kk):
                e.dma_start(out=T["carry_all"][:, kk * 10:(kk + 1) * 10],
                            in_=T["ag_out_d"][kk]).then_inc(dG.h, 16)
            P.on("gpsimd", g3)
            dG.inc(16)
        n2 = dG.n
        P.on("gpsimd", lambda e, n=n2: e.wait_ge(dG.h, n))
        vag = P.signal_S("gpsimd")

        P.wait_S("vector", vag)
        def vc1(e):
            e.tensor_tensor(T["carry_all"][:, 0:80], T["carry_all"][:, 0:80],
                            T["coremask80"][:, 0:80], AOT.mult)
        P.on("vector", vc1)
        P.guard("vector")
        def vc2(e):
            e.tensor_reduce(
                T["carry"][:, 0:10],
                T["carry_all"][:, 0:80].rearrange("p (k rc) -> p rc k", k=8),
                AXL.X, AOT.add)
        P.on("vector", vc2)
        P.guard("vector")
        for r, t_ in enumerate((pre_, pim_)):
            for bc in range(NB5):
                mw = 128 if bc < 4 else 1
                def va(e, r=r, bc=bc, mw=mw, t_=t_):
                    e.tensor_scalar(t_[0:mw, bc * 512:(bc + 1) * 512],
                                    t_[0:mw, bc * 512:(bc + 1) * 512],
                                    T["carry"][0:mw, r * NB5 + bc: r * NB5 + bc + 1],
                                    None, AOT.add)
                P.on("vector", va)

        for h_re, h_im, q_re, q_im, s1_, s2_ in (
            (pre_[:, 0:2048], pim_[:, 0:2048], fqre[:, 0:2048], fqim[:, 0:2048],
             spB[:, 0:2048], spB2[:, 0:2048]),
            (pre_[0:1, 2048:2560], pim_[0:1, 2048:2560], fqre[0:1, 2048:2560],
             fqim[0:1, 2048:2560], spB[0:1, 0:512], spB2[0:1, 0:512]),
        ):
            def vG(e, hr=h_re, hi=h_im, qr=q_re, qi_=q_im, s1=s1_, s2=s2_):
                e.tensor_tensor(s1, hi, qr, AOT.mult)
                e.tensor_tensor(s2, hr, qi_, AOT.mult)
                e.tensor_tensor(hr, hr, qr, AOT.mult)
                e.tensor_tensor(hi, hi, qi_, AOT.mult)
                e.tensor_tensor(hr, hr, hi, AOT.add)
                e.tensor_tensor(hi, s1, s2, AOT.subtract)
            P.on("vector", vG)
        bG = P.barrier(("vector", "tensor", "scalar", "sync"))

        P.on("tensor", lambda e, n=nIR: e.wait_ge(dW.h, n))
        res_base = evS.n
        for i in range(4):
            tc = 4 * b + i
            par = i % 2
            dXp = P.dma_sem(f"dXrp{par}")
            if i >= 2:
                P.on("sync", lambda e, v=res_base + 2 * (i - 1): e.wait_ge(evS.h, v))
            def fx(e, tc=tc, par=par, dXp=dXp):
                e.dma_start(out=xbuf[:, par * D:(par + 1) * D],
                            in_=x2_dram[tc * 128:(tc + 1) * 128, :]).then_inc(dXp.h, 16)
            P.on("sync", fx)
            nx = dXp.inc(16)
            for fh in range(2):
                ps = pb[fh]
                P.on("tensor", lambda e, v=max(0, evS.n - 1): e.wait_ge(evS.h, v))
                mv = mmS.inc()
                for (g_t, ir_t) in ((pre_, T["ir_re"]), (pim_, T["ir_im"])):
                    for bc in range(NB5):
                        mw = 128 if bc < 4 else 1
                        def mm(e, g_t=g_t, ir_t=ir_t, bc=bc, i=i, fh=fh, mw=mw, ps=ps,
                               first=(g_t is pre_ and bc == 0),
                               last=(g_t is pim_ and bc == NB5 - 1)):
                            m = e.matmul(ps[:, 0:512],
                                         g_t[0:mw, bc * 512 + i * 128: bc * 512 + i * 128 + 128],
                                         ir_t[0:mw, bc * D + fh * 512: bc * D + (fh + 1) * 512],
                                         start=first, stop=last)
                            if last:
                                m.then_inc(mmS.h, 1)
                        P.on("tensor", mm)

                def ev(e, i=i, fh=fh, ps=ps, mv=mv, nx=nx, dXp=dXp):
                    e.wait_ge(mmS.h, mv)
                    e.wait_ge(dXp.h, nx)
                    e.scalar_tensor_tensor(
                        x3b[:, i * D + fh * 512: i * D + (fh + 1) * 512],
                        xbuf[:, (i % 2) * D + fh * 512: (i % 2) * D + (fh + 1) * 512],
                        2.0, ps[:, 0:512], AOT.mult, AOT.add)
                P.on("vector", ev)
                evS.inc()
                P.on("vector", lambda e: e.drain().then_inc(evS.h, 1))
        bres = P.barrier(("vector", "tensor", "scalar", "sync"))
        emit_ln(P, x3b, x3b, T["scr_a"], T["mean_s"], T["scr_b"], nchunk=4)
        bln = P.barrier(("vector", "scalar"))
        P.wait_S("gpsimd", bln)
        st = P.dma_sem("dOutL1a")
        def go(e, b=b):
            e.dma_start(
                out=out_dram[b * 512:(b + 1) * 512, :].rearrange("(c p) m -> p c m", p=128),
                in_=x3b[:, 0:4 * D].rearrange("p (c m) -> p c m", c=4),
            ).then_inc(st.h, 16)
        P.on("gpsimd", go)
        ns = st.inc(16)
        P.on("gpsimd", lambda e, n=ns: e.wait_ge(st.h, n))
        P.signal_S("gpsimd")
        P.barrier(("sync", "tensor", "vector", "scalar", "gpsimd"))


def emit_l1f_fold(P, T):
    mmS = P.aux_sem("fold_mm")
    pb = T["psum"]
    wb = T["B"]
    FC = 32

    for pss in range(3):
        cons = P.aux_sem(f"fold_cons{pss}")
        for fc in range(FC):
            par = fc % 2
            dFp = P.dma_sem(f"dFoldp{par}")
            if fc >= 2:
                P.on("sync", lambda e, v=fc - 1, cons=cons: e.wait_ge(cons.h, v))

            def f(e, fc=fc, par=par, dFp=dFp, pss=pss):
                if pss < 2:
                    e.dma_start(out=wb[:, par * 2048: par * 2048 + 1024],
                                in_=T["w1T_d"][fc * 128:(fc + 1) * 128, :]).then_inc(dFp.h, 16)
                    e.dma_start(out=wb[:, par * 2048 + 1024: par * 2048 + 1536],
                                in_=T["w2_d"][fc * 128:(fc + 1) * 128, pss * 512:(pss + 1) * 512]
                                ).then_inc(dFp.h, 16)
                else:
                    e.dma_start(out=wb[:, par * 2048: par * 2048 + 1024],
                                in_=T["w2_d"][fc * 128:(fc + 1) * 128, :]).then_inc(dFp.h, 16)
            P.on("sync", f)
            nf = dFp.inc(32 if pss < 2 else 16)
            P.on("tensor", lambda e, n=nf, dFp=dFp: e.wait_ge(dFp.h, n))
            cons.inc()
            if pss < 2:
                for dc in range(DC):
                    def mm(e, fc=fc, dc=dc, par=par, ps=pb[dc], cons=cons):
                        m = e.matmul(ps[:, 0:512],
                                     wb[:, par * 2048 + dc * 128: par * 2048 + (dc + 1) * 128],
                                     wb[:, par * 2048 + 1024: par * 2048 + 1536],
                                     start=(fc == 0), stop=(fc == FC - 1))
                        if dc == DC - 1:
                            m.then_inc(cons.h, 1)
                    P.on("tensor", mm)
            else:
                for half in range(2):
                    def mmb(e, fc=fc, par=par, half=half, ps=pb[half], cons=cons):
                        m = e.matmul(ps[0:1, 0:512],
                                     T["b1col"][:, fc:fc + 1],
                                     wb[:, par * 2048 + half * 512: par * 2048 + (half + 1) * 512],
                                     start=(fc == 0), stop=(fc == FC - 1))
                        if half == 1:
                            m.then_inc(cons.h, 1)
                    P.on("tensor", mmb)
        tmm = P.signal_S("tensor")
        P.wait_S("vector", tmm)
        if pss < 2:
            def ev(e, pss=pss):
                for dc in range(DC):
                    e.tensor_copy(
                        T["foldout"][:, dc * 1024 + pss * 512: dc * 1024 + (pss + 1) * 512],
                        pb[dc][:, 0:512])
            P.on("vector", ev)
        else:
            def ev(e):
                e.tensor_tensor(T["bbrow"][0:1, 0:512], pb[0][0:1, 0:512],
                                T["b2row"][0:1, 0:512], AOT.add)
                e.tensor_tensor(T["bbrow"][0:1, 512:1024], pb[1][0:1, 0:512],
                                T["b2row"][0:1, 512:1024], AOT.add)
            P.on("vector", ev)
        vev = P.signal_S("vector")
        P.wait_S("tensor", vev)
        P.wait_S("sync", vev)

    P.wait_S("gpsimd", P.signal_S("vector"))
    dG = P.dma_sem("dFoldAg")
    def g1(e):
        e.dma_start(out=T["ww_d"][0:1024, :].rearrange("(c p) m -> p c m", p=128),
                    in_=T["foldout"][:, 0:DC * D].rearrange("p (c m) -> p c m", c=DC)
                    ).then_inc(dG.h, 16)
        e.dma_start(out=T["ww_d"][1024:1025, :], in_=T["bbrow"][0:1, 0:1024],
                    single_packet=True).then_inc(dG.h, 16)
    P.on("gpsimd", g1)
    n1 = dG.inc(32)
    P.on("gpsimd", lambda e, n=n1: e.wait_ge(dG.h, n))
    agS = P.aux_sem("agW")
    def g2(e):
        e.collective_compute(
            "AllGather", AOT.bypass, replica_groups=[list(range(NCORES))],
            ins=[T["ww_d"].opt()],
            outs=[T["ww_all_d"].opt()],
        ).then_inc(agS.h, 1)
    P.on("gpsimd", g2)
    agS.inc()


def emit_l1ffn(P, T, x3_dram, out_dram, y1_dram):
    pb = T["psum"]
    mmS = P.aux_sem("l1f_mm")
    evS = P.aux_sem("l1f_ev")
    A, Bb, y1T = T["A"], T["B"], T["y1T"]
    qkv = A[:, 0:12288]
    scr4 = A[:, 12288:16384]
    oT = A[:, 12288:16384]
    wm = Bb[:, 0:4096]
    x3T = Bb[:, 4096:8192]
    xst = Bb[:, 8192:12288]
    ost = Bb[:, 12288:16384]
    sst = T["scores"]
    ms = T["mean_s"]
    lnld = P.dma_sem("dLng")

    def load_ln_tiles(g_d, b_d):
        def f(e, g_d=g_d, b_d=b_d):
            e.dma_start(out=Bb[:, 0:1024], in_=g_d).then_inc(lnld.h, 16)
            e.dma_start(out=Bb[:, 1024:2048], in_=b_d).then_inc(lnld.h, 16)
        P.on("sync", f)
        return lnld.inc(32)

    def wslice(par, dc):
        return wm[:, par * 2048 + dc * 256: par * 2048 + (dc + 1) * 256] if False \
            else wm[:, par * 2048:(par + 1) * 2048].rearrange(
                "p (c m) -> p c m", c=DC)[:, dc, :]

    for j in range(4):
        P.barrier(("sync", "tensor", "vector", "scalar", "gpsimd"))
        for b_ in range(4):
            tc = 4 * b_ + j
            dXp = P.dma_sem(f"dX3p{b_ % 2}")
            def fx(e, tc=tc, b_=b_, dXp=dXp):
                e.dma_start(out=xst[:, b_ * D:(b_ + 1) * D],
                            in_=x3_dram[tc * 128:(tc + 1) * 128, :]).then_inc(dXp.h, 16)
            P.on("sync", fx)
            nx = dXp.inc(16)
            P.on("tensor", lambda e, n=nx, dXp=dXp: e.wait_ge(dXp.h, n))
            for dc in range(DC):
                ps = pb[dc % 2]
                P.on("tensor", lambda e, v=max(0, evS.n - 1): e.wait_ge(evS.h, v))
                mmS.inc()
                def tr(e, dc=dc, b_=b_, ps=ps):
                    e.transpose(ps[:, 0:128],
                                xst[:, b_ * D + dc * 128: b_ * D + (dc + 1) * 128],
                                T["ident"][:]).then_inc(mmS.h, 1)
                P.on("tensor", tr)
                def ev(e, dc=dc, b_=b_, ps=ps, mv=mmS.n):
                    e.wait_ge(mmS.h, mv)
                    e.copy(x3T[:, dc * 512 + b_ * 128: dc * 512 + (b_ + 1) * 128],
                           ps[:, 0:128])
                P.on("scalar", ev)
                evS.inc()
                P.on("scalar", lambda e: e.drain().then_inc(evS.h, 1))
        P.barrier(("tensor", "scalar", "sync"))

        wcons = P.aux_sem(f"l1f_wcons{j}")
        for p12 in range(12):
            mh6, dch = divmod(p12, 2)
            par = p12 % 2
            dWp = P.dma_sem(f"dWmp{par}")
            if p12 >= 2:
                P.on("sync", lambda e, v=p12 - 1, wcons=wcons: e.wait_ge(wcons.h, v))
            def fw(e, mh6=mh6, dch=dch, par=par, dWp=dWp):
                e.dma_start(
                    out=wm[:, par * 2048:(par + 1) * 2048]
                        .rearrange("p (c m) -> p c m", c=4),
                    in_=T["l1f_wT_d"][dch * 512:(dch + 1) * 512,
                                      mh6 * 512:(mh6 + 1) * 512]
                        .rearrange("(c p) m -> p c m", p=128),
                ).then_inc(dWp.h, 16)
                if dch == 1:
                    e.dma_start(out=T["brow_slot"][0:1, 0:512],
                                in_=T["l1f_brow_d"][0:1, mh6 * 512:(mh6 + 1) * 512],
                                single_packet=True).then_inc(dWp.h, 16)
            P.on("sync", fw)
            nw = dWp.inc(32 if dch == 1 else 16)
            P.on("tensor", lambda e, n=nw, dWp=dWp: e.wait_ge(dWp.h, n))
            for uc in range(4):
                ps = pb[4 + uc % 4]
                if dch == 0:
                    P.on("tensor", lambda e, v=max(0, evS.n - 3): e.wait_ge(evS.h, v))
                mmS.inc()
                for dc4 in range(4):
                    def mm(e, uc=uc, dc4=dc4, dch=dch, par=par, ps=ps,
                           fin=(dch == 1)):
                        m = e.matmul(ps[:, 0:512],
                                     x3T[:, (dch * 4 + dc4) * 512 + uc * 128:
                                         (dch * 4 + dc4) * 512 + (uc + 1) * 128],
                                     wm[:, par * 2048 + dc4 * 512:
                                        par * 2048 + (dc4 + 1) * 512],
                                     start=(dch == 0 and dc4 == 0), stop=False)
                        if not fin and dc4 == 3:
                            m.then_inc(mmS.h, 1)
                    P.on("tensor", mm)
                if dch == 1:
                    def mmb(e, ps=ps):
                        e.matmul(ps[:, 0:512], T["ones_row"][0:1, 0:128],
                                 T["brow_slot"][0:1, 0:512],
                                 start=False, stop=True).then_inc(mmS.h, 1)
                    P.on("tensor", mmb)
                if dch == 1:
                    def ev(e, uc=uc, mh6=mh6, ps=ps, mv=mmS.n):
                        e.wait_ge(mmS.h, mv)
                        e.copy(qkv[:, uc * 3072 + mh6 * 512:
                                   uc * 3072 + (mh6 + 1) * 512], ps[:, 0:512])
                    P.on("scalar", ev)
                    evS.inc()
                    P.on("scalar", lambda e: e.drain().then_inc(evS.h, 1))
            wcons.inc()
            P.on("tensor", lambda e, wcons=wcons: e.drain().then_inc(wcons.h, 1))
        P.barrier(("tensor", "scalar", "vector"))

        kall = qkv.rearrange("p (m w) -> p m w", m=4)[:, :, 1024:2048]
        vall = qkv.rearrange("p (m w) -> p m w", m=4)[:, :, 2048:3072]
        for l in range(4):
            def vsc(e, l=l):
                qb = qkv[:, l * 3072: l * 3072 + 1024]
                e.tensor_tensor(
                    scr4[:, 0:4096].rearrange("p (m d) -> p m d", m=4),
                    qb.rearrange("p (a d) -> p a d", a=1).broadcast_to([128, 4, 1024]),
                    kall, AOT.mult)
                e.tensor_reduce(
                    sst[:, l * 16:(l + 1) * 16].rearrange("p (h m) -> p h m", h=4),
                    scr4[:, 0:4096].rearrange("p (m h d) -> p h m d", m=4, h=4),
                    AXL.X, AOT.add)
            P.on("vector", vsc)
        P.guard("vector")
        P.on("vector", lambda e: e.tensor_scalar(sst[:, 0:64], sst[:, 0:64],
                                                 1.0 / 16.0, None, AOT.mult))
        P.guard("vector")
        P.on("vector", lambda e: e.tensor_reduce(
            ms[:, 32:48], sst[:, 0:64].rearrange("p (g m) -> p g m", m=4),
            AXL.X, AOT.max))
        P.guard("vector")
        P.on("vector", lambda e: e.tensor_tensor(
            sst[:, 0:64].rearrange("p (g m) -> p g m", m=4),
            sst[:, 0:64].rearrange("p (g m) -> p g m", m=4),
            ms[:, 32:48].unsqueeze(2).broadcast_to([128, 16, 4]), AOT.subtract))
        vready = P.signal_S("vector")
        P.wait_S("scalar", vready)
        P.on("scalar", lambda e: e.activation(sst[:, 0:64], sst[:, 0:64], AFT.Exp))
        sdone = P.signal_S("scalar")
        P.wait_S("vector", sdone)
        P.on("vector", lambda e: e.tensor_reduce(
            ms[:, 32:48], sst[:, 0:64].rearrange("p (g m) -> p g m", m=4),
            AXL.X, AOT.add))
        P.guard("vector")
        P.on("vector", lambda e: e.reciprocal(ms[:, 32:48], ms[:, 32:48]))
        P.guard("vector")
        P.on("vector", lambda e: e.tensor_tensor(
            sst[:, 0:64].rearrange("p (g m) -> p g m", m=4),
            sst[:, 0:64].rearrange("p (g m) -> p g m", m=4),
            ms[:, 32:48].unsqueeze(2).broadcast_to([128, 16, 4]), AOT.mult))
        P.guard("vector")
        for l in range(4):
            def vo(e, l=l):
                att = sst[:, l * 16:(l + 1) * 16]
                e.tensor_tensor(
                    scr4[:, 0:4096].rearrange("p (m h d) -> p m h d", m=4, h=4),
                    vall.rearrange("p m (h d) -> p m h d", h=4),
                    att.rearrange("p (h m) -> p m h", h=4).unsqueeze(3)
                       .broadcast_to([128, 4, 4, 256]),
                    AOT.mult)
                e.tensor_reduce(
                    ost[:, l * D:(l + 1) * D],
                    scr4[:, 0:4096].rearrange("p (m d) -> p d m", m=4),
                    AXL.X, AOT.add)
            P.on("vector", vo)
        P.barrier(("vector", "tensor", "scalar"))

        for b_ in range(4):
            for dc in range(DC):
                ps = pb[dc % 2]
                P.on("tensor", lambda e, v=max(0, evS.n - 1): e.wait_ge(evS.h, v))
                mmS.inc()
                def tr(e, dc=dc, b_=b_, ps=ps):
                    e.transpose(ps[:, 0:128],
                                ost[:, b_ * D + dc * 128: b_ * D + (dc + 1) * 128],
                                T["ident"][:]).then_inc(mmS.h, 1)
                P.on("tensor", tr)
                def ev(e, dc=dc, b_=b_, ps=ps, mv=mmS.n):
                    e.wait_ge(mmS.h, mv)
                    e.copy(oT[:, dc * 512 + b_ * 128: dc * 512 + (b_ + 1) * 128],
                           ps[:, 0:128])
                P.on("scalar", ev)
                evS.inc()
                P.on("scalar", lambda e: e.drain().then_inc(evS.h, 1))
        P.barrier(("tensor", "scalar", "sync"))
        ocons = P.aux_sem(f"l1f_ocons{j}")
        for p4 in range(4):
            mh2, dch = divmod(p4, 2)
            par = p4 % 2
            dWp = P.dma_sem(f"dOwp{par}")
            if p4 >= 2:
                P.on("sync", lambda e, v=p4 - 1, ocons=ocons: e.wait_ge(ocons.h, v))
            def fw(e, mh2=mh2, dch=dch, par=par, dWp=dWp):
                e.dma_start(
                    out=wm[:, par * 2048:(par + 1) * 2048]
                        .rearrange("p (c m) -> p c m", c=4),
                    in_=T["l1f_owT_d"][dch * 512:(dch + 1) * 512,
                                       mh2 * 512:(mh2 + 1) * 512]
                        .rearrange("(c p) m -> p c m", p=128),
                ).then_inc(dWp.h, 16)
                if dch == 1:
                    e.dma_start(out=T["brow_slot"][0:1, 0:512],
                                in_=T["l1f_obrow_d"][0:1, mh2 * 512:(mh2 + 1) * 512],
                                single_packet=True).then_inc(dWp.h, 16)
            P.on("sync", fw)
            nw = dWp.inc(32 if dch == 1 else 16)
            P.on("tensor", lambda e, n=nw, dWp=dWp: e.wait_ge(dWp.h, n))
            for uc in range(4):
                ps = pb[4 + uc % 4]
                if dch == 0:
                    P.on("tensor", lambda e, v=max(0, evS.n - 3): e.wait_ge(evS.h, v))
                mmS.inc()
                for dc4 in range(4):
                    def mm(e, uc=uc, dc4=dc4, dch=dch, par=par, ps=ps,
                           fin=(dch == 1)):
                        m = e.matmul(ps[:, 0:512],
                                     oT[:, (dch * 4 + dc4) * 512 + uc * 128:
                                        (dch * 4 + dc4) * 512 + (uc + 1) * 128],
                                     wm[:, par * 2048 + dc4 * 512:
                                        par * 2048 + (dc4 + 1) * 512],
                                     start=(dch == 0 and dc4 == 0), stop=False)
                        if not fin and dc4 == 3:
                            m.then_inc(mmS.h, 1)
                    P.on("tensor", mm)
                if dch == 1:
                    def mmb(e, ps=ps):
                        e.matmul(ps[:, 0:512], T["ones_row"][0:1, 0:128],
                                 T["brow_slot"][0:1, 0:512],
                                 start=False, stop=True).then_inc(mmS.h, 1)
                    P.on("tensor", mmb)
                if dch == 1:
                    def ev(e, uc=uc, mh2=mh2, ps=ps, mv=mmS.n):
                        e.wait_ge(mmS.h, mv)
                        e.tensor_tensor(
                            xst[:, uc * D + mh2 * 512: uc * D + (mh2 + 1) * 512],
                            ps[:, 0:512],
                            xst[:, uc * D + mh2 * 512: uc * D + (mh2 + 1) * 512],
                            AOT.add)
                    P.on("vector", ev)
                    evS.inc()
                    P.on("vector", lambda e: e.drain().then_inc(evS.h, 1))
            ocons.inc()
            P.on("tensor", lambda e, ocons=ocons: e.drain().then_inc(ocons.h, 1))
        P.barrier(("vector", "tensor", "scalar", "sync"))
        nln = load_ln_tiles(T["ln1g_d"], T["ln1b_d"])
        P.on("vector", lambda e, n=nln: e.wait_ge(lnld.h, n))
        emit_ln(P, xst, xst, T["scr_a"], ms, T["scr_b"],
                gamma=Bb[:, 0:1024], beta=Bb[:, 1024:2048], nchunk=4)
        bln = P.barrier(("vector", "scalar", "tensor"))
        for b_ in range(4):
            for dc in range(DC):
                ps = pb[dc % 2]
                P.on("tensor", lambda e, v=max(0, evS.n - 1): e.wait_ge(evS.h, v))
                mmS.inc()
                def tr(e, dc=dc, b_=b_, ps=ps):
                    e.transpose(ps[:, 0:128],
                                xst[:, b_ * D + dc * 128: b_ * D + (dc + 1) * 128],
                                T["ident"][:]).then_inc(mmS.h, 1)
                P.on("tensor", tr)
                def ev(e, dc=dc, b_=b_, j=j, ps=ps, mv=mmS.n):
                    e.wait_ge(mmS.h, mv)
                    e.copy(y1T[:, dc * TOK + (4 * b_ + j) * 128:
                               dc * TOK + (4 * b_ + j + 1) * 128], ps[:, 0:128])
                P.on("scalar", ev)
                evS.inc()
                P.on("scalar", lambda e: e.drain().then_inc(evS.h, 1))
        P.wait_S("gpsimd", bln)
        dY = P.dma_sem("dY1")
        for b_ in range(4):
            def gy(e, b_=b_, j=j, dY=dY):
                e.dma_start(out=y1_dram[(4 * b_ + j) * 128:(4 * b_ + j + 1) * 128, :],
                            in_=xst[:, b_ * D:(b_ + 1) * D]).then_inc(dY.h, 16)
            P.on("gpsimd", gy)
            dY.inc(16)
        P.on("gpsimd", lambda e, n=dY.n: e.wait_ge(dY.h, n))
        P.signal_S("gpsimd")
    P.barrier(("sync", "tensor", "vector", "scalar", "gpsimd"))

    lg = T["logits"]
    oh = T["onehot"]
    ohb = T["onehot_bf"]
    gk = T["gatekeep"]
    gx = T["scr_a"]
    for tc in range(TC):
        ps = pb[4 + tc % 2]
        P.on("tensor", lambda e, v=max(0, evS.n - 1): e.wait_ge(evS.h, v))
        mmS.inc()
        for dc in range(DC):
            def mm(e, tc=tc, dc=dc, ps=ps):
                m = e.matmul(ps[:, 0:8],
                             y1T[:, dc * TOK + tc * 128: dc * TOK + (tc + 1) * 128],
                             T["wg_sb"][:, dc * 8:(dc + 1) * 8],
                             start=(dc == 0), stop=(dc == DC - 1))
                if dc == DC - 1:
                    m.then_inc(mmS.h, 1)
            P.on("tensor", mm)
        def ev(e, tc=tc, ps=ps, mv=mmS.n):
            e.wait_ge(mmS.h, mv)
            e.copy(lg[:, tc * 8:(tc + 1) * 8], ps[:, 0:8])
        P.on("scalar", ev)
        evS.inc()
        P.on("scalar", lambda e: e.drain().then_inc(evS.h, 1))
    P.barrier(("tensor", "scalar", "vector"))

    mx = ms[:, 0:16]
    for tc in range(TC):
        def v1(e, tc=tc):
            e.tensor_reduce(mx[:, tc:tc + 1], lg[:, tc * 8:(tc + 1) * 8], AXL.X, AOT.max)
        P.on("vector", v1)
    P.guard("vector")
    for tc in range(TC):
        def v2(e, tc=tc):
            e.tensor_scalar(oh[:, tc * 8:(tc + 1) * 8], lg[:, tc * 8:(tc + 1) * 8],
                            mx[:, tc:tc + 1], None, AOT.is_ge)
        P.on("vector", v2)
    for tc in range(TC):
        def v3(e, tc=tc):
            e.tensor_tensor(oh[:, tc * 8:(tc + 1) * 8], oh[:, tc * 8:(tc + 1) * 8],
                            T["reviota"][:, 0:8], AOT.mult)
        P.on("vector", v3)
    P.guard("vector")
    for tc in range(TC):
        def v4(e, tc=tc):
            e.tensor_reduce(ms[:, 16 + tc:17 + tc], oh[:, tc * 8:(tc + 1) * 8],
                            AXL.X, AOT.max)
        P.on("vector", v4)
    P.guard("vector")
    P.on("vector", lambda e: e.tensor_scalar(ms[:, 16:32], ms[:, 16:32],
                                             -1.0, 8.0, AOT.mult, AOT.add))
    P.guard("vector")
    for tc in range(TC):
        def v5(e, tc=tc):
            e.tensor_tensor(oh[:, tc * 8:(tc + 1) * 8], T["iota8"][:, 0:8],
                            ms[:, 16 + tc:17 + tc].broadcast_to([128, 8]), AOT.is_equal)
        P.on("vector", v5)
    for tc in range(TC):
        def v6(e, tc=tc):
            e.tensor_scalar(gx[:, tc * 8:(tc + 1) * 8], lg[:, tc * 8:(tc + 1) * 8],
                            mx[:, tc:tc + 1], None, AOT.subtract)
        P.on("vector", v6)
    vex = P.signal_S("vector")
    P.wait_S("scalar", vex)
    P.on("scalar", lambda e: e.activation(gx[:, 0:128], gx[:, 0:128], AFT.Exp))
    sex = P.signal_S("scalar")
    P.wait_S("vector", sex)
    for tc in range(TC):
        def v7(e, tc=tc):
            e.tensor_reduce(gk[:, tc:tc + 1], gx[:, tc * 8:(tc + 1) * 8], AXL.X, AOT.add)
        P.on("vector", v7)
    P.guard("vector")
    P.on("vector", lambda e: e.reciprocal(gk[:, 0:16], gk[:, 0:16]))
    P.on("vector", lambda e: e.tensor_copy(ohb[:, 0:128], oh[:, 0:128]))
    P.barrier(("vector", "tensor"))

    cum = T["cum"]
    for b_ in range(4):
        for cj in range(4):
            ps = pb[cj % 4]
            P.on("tensor", lambda e, v=max(0, evS.n - 3): e.wait_ge(evS.h, v))
            for ci in range(cj + 1):
                def mm(e, b_=b_, cj=cj, ci=ci, ps=ps):
                    lmat = T["trimat"][:, 0:128] if ci == cj else T["trimat"][:, 128:256]
                    e.matmul(ps[:, 0:8], lmat,
                             ohb[:, (4 * b_ + ci) * 8:(4 * b_ + ci + 1) * 8],
                             start=(ci == 0), stop=(ci == cj))
                P.on("tensor", mm)
            mmS.inc()
            P.on("tensor", lambda e: e.drain().then_inc(mmS.h, 1))
            def ev(e, b_=b_, cj=cj, ps=ps, mv=mmS.n):
                e.wait_ge(mmS.h, mv)
                e.copy(cum[:, (4 * b_ + cj) * 8:(4 * b_ + cj + 1) * 8], ps[:, 0:8])
            P.on("scalar", ev)
            evS.inc()
            P.on("scalar", lambda e: e.drain().then_inc(evS.h, 1))
    bcnt = P.barrier(("tensor", "scalar", "vector"))

    P.wait_S("gpsimd", bcnt)
    dC = P.dma_sem("dCnt")
    for b_ in range(4):
        def gc(e, b_=b_):
            e.dma_start(out=T["cnt_d"][b_:b_ + 1, :],
                        in_=cum[127:128, (4 * b_ + 3) * 8:(4 * b_ + 4) * 8],
                        single_packet=True).then_inc(dC.h, 16)
        P.on("gpsimd", gc)
        dC.inc(16)
    P.on("gpsimd", lambda e, n=dC.n: e.wait_ge(dC.h, n))
    agC = P.aux_sem("agC")
    def g2(e):
        e.collective_compute(
            "AllGather", AOT.bypass, replica_groups=[list(range(NCORES))],
            ins=[T["cnt_d"].opt()], outs=[T["cnt_all_d"].opt()],
        ).then_inc(agC.h, 1)
    P.on("gpsimd", g2)
    P.on("gpsimd", lambda e, n=agC.inc(): e.wait_ge(agC.h, n))
    def g3(e):
        e.dma_start(out=T["cntrow"][0:1, 0:256],
                    in_=T["cnt_all_d"].rearrange("k b e -> (k b e)"),
                    single_packet=True).then_inc(dC.h, 16)
    P.on("gpsimd", g3)
    P.on("gpsimd", lambda e, n=dC.inc(16): e.wait_ge(dC.h, n))
    vag = P.signal_S("gpsimd")

    dMM = P.dma_sem("dMoemask")
    def gmm(e, dMM=dMM):
        e.dma_start(out=T["moemask"], in_=T["moemask_dram"],
                    single_packet=True).then_inc(dMM.h, 16)
    P.on("gpsimd", gmm)
    P.on("gpsimd", lambda e, n=dMM.inc(16): e.wait_ge(dMM.h, n))
    P.wait_S("vector", vag)
    for b_ in range(4):
        def vb(e, b_=b_):
            e.tensor_tensor(T["cntrow"][0:1, 288:544], T["cntrow"][0:1, 0:256],
                            T["moemask"][0:1, b_ * 256:(b_ + 1) * 256], AOT.mult)
        P.on("vector", vb)
        P.guard("vector")
        def vb2(e, b_=b_):
            e.tensor_reduce(T["cntrow"][0:1, 256 + b_ * 8:256 + (b_ + 1) * 8],
                            T["cntrow"][0:1, 288:544].rearrange("p (k e) -> p e k", e=8),
                            AXL.X, AOT.add)
        P.on("vector", vb2)
        P.guard("vector")
    vbase = P.signal_S("vector")
    P.wait_S("gpsimd", vbase)
    def g4(e):
        e.dma_start(out=T["base_d"], in_=T["cntrow"][0:1, 256:288],
                    single_packet=True).then_inc(dC.h, 16)
    P.on("gpsimd", g4)
    P.on("gpsimd", lambda e, n=dC.inc(16): e.wait_ge(dC.h, n))
    def g5(e):
        e.dma_start(out=T["basebc"][:, 0:32].rearrange("p (a m) -> p a m", a=1),
                    in_=T["base_d"].partition_broadcast(128)).then_inc(dC.h, 16)
    P.on("gpsimd", g5)
    P.on("gpsimd", lambda e, n=dC.inc(16): e.wait_ge(dC.h, n))
    vbc = P.signal_S("gpsimd")

    P.wait_S("vector", vbc)
    for tc in range(TC):
        b_ = tc // 4
        def v8(e, tc=tc, b_=b_):
            e.tensor_tensor(gx[:, tc * 8:(tc + 1) * 8], cum[:, tc * 8:(tc + 1) * 8],
                            T["basebc"][:, b_ * 8:(b_ + 1) * 8], AOT.add)
        P.on("vector", v8)
    P.guard("vector")
    for tc in range(TC):
        def v8b(e, tc=tc):
            e.tensor_tensor(gx[:, tc * 8:(tc + 1) * 8], gx[:, tc * 8:(tc + 1) * 8],
                            oh[:, tc * 8:(tc + 1) * 8], AOT.mult)
        P.on("vector", v8b)
    P.guard("vector")
    for tc in range(TC):
        def v9(e, tc=tc):
            e.tensor_reduce(ms[:, 16 + tc:17 + tc], gx[:, tc * 8:(tc + 1) * 8],
                            AXL.X, AOT.add)
        P.on("vector", v9)
    P.guard("vector")
    P.on("vector", lambda e: e.tensor_scalar(ms[:, 48:64], ms[:, 16:32],
                                             2048.0, None, AOT.is_le))
    P.guard("vector")
    P.on("vector", lambda e: e.tensor_tensor(gk[:, 0:16], gk[:, 0:16],
                                             ms[:, 48:64], AOT.mult))
    P.guard("vector")
    for tc in range(TC):
        def v10(e, tc=tc):
            e.tensor_scalar(T["coef"][:, tc * 8:(tc + 1) * 8], oh[:, tc * 8:(tc + 1) * 8],
                            gk[:, tc:tc + 1], None, AOT.mult)
        P.on("vector", v10)
    broute_t = P.barrier(("vector", "tensor", "scalar", "gpsimd", "sync"))
    if "dbg_route" in T:
        dd = P.dma_sem("dDbgR")
        def gd(e, dd=dd):
            e.dma_start(out=T["dbg_route"][0:128, :], in_=T["basebc"][:, 0:32]).then_inc(dd.h, 16)
            e.dma_start(out=T["dbg_route"][128:256, 0:16], in_=ms[:, 16:32]).then_inc(dd.h, 16)
            e.dma_start(out=T["dbg_route"][128:256, 16:32], in_=ms[:, 48:64]).then_inc(dd.h, 16)
            e.dma_start(out=T["dbg_route"][256:384, 0:16], in_=gk[:, 0:16]).then_inc(dd.h, 16)
            e.dma_start(out=T["dbg_route"][384:512, 0:8],
                        in_=cum[:, (4 * 3 + 3) * 8:(4 * 3 + 4) * 8]).then_inc(dd.h, 16)
        P.on("gpsimd", gd)
        P.on("gpsimd", lambda e, n=dd.inc(80): e.wait_ge(dd.h, n))
        P.signal_S("gpsimd")
        P.barrier(("vector", "tensor", "scalar", "gpsimd", "sync"))

    acc = A
    agW = P.aux_sem("agW")
    P.on("sync", lambda e: e.wait_ge(agW.h, 1))
    econs = P.aux_sem("l1f_econs")
    for ex in range(NCORES):
        epar = 0
        wwbuf = Bb[:, 0:8192]
        bbb = T["bbbuf"]
        if ex >= 1:
            P.on("sync", lambda e, v=ex: e.wait_ge(econs.h, v))
        dEp = P.dma_sem(f"dExp{epar}")
        def fw(e, ex=ex, dEp=dEp, wwbuf=wwbuf, bbb=bbb):
            e.dma_start(
                out=wwbuf.rearrange("p (c m) -> p c m", c=DC),
                in_=T["ww_all_d"][ex][0:1024, :].rearrange("(c p) m -> p c m", p=128),
            ).then_inc(dEp.h, 16)
            e.dma_start(out=bbb[0:1, 0:1024],
                        in_=T["ww_all_d"][ex][1024:1025, :],
                        single_packet=True).then_inc(dEp.h, 16)
        P.on("sync", fw)
        nw = dEp.inc(32)
        P.on("tensor", lambda e, n=nw, dEp=dEp: e.wait_ge(dEp.h, n))
        for tc in range(TC):
            for mh in range(2):
                ps = pb[(tc * 2 + mh) % 4]
                P.on("tensor", lambda e, v=max(0, evS.n - 3): e.wait_ge(evS.h, v))
                for dc in range(DC):
                    def mm(e, tc=tc, mh=mh, dc=dc, ps=ps, wwbuf=wwbuf):
                        e.matmul(ps[:, 0:512],
                                 y1T[:, dc * TOK + tc * 128: dc * TOK + (tc + 1) * 128],
                                 wwbuf[:, dc * 1024 + mh * 512: dc * 1024 + (mh + 1) * 512],
                                 start=(dc == 0), stop=False)
                    P.on("tensor", mm)
                mmS.inc()
                def mmb(e, mh=mh, ps=ps, bbb=bbb):
                    e.matmul(ps[:, 0:512], T["ones_row"][0:1, 0:128],
                             bbb[0:1, mh * 512:(mh + 1) * 512],
                             start=False, stop=True).then_inc(mmS.h, 1)
                P.on("tensor", mmb)
                def ev(e, ex=ex, tc=tc, mh=mh, ps=ps, mv=mmS.n):
                    e.wait_ge(mmS.h, mv)
                    dst = acc[:, tc * D + mh * 512: tc * D + (mh + 1) * 512]
                    if ex == 0:
                        e.tensor_scalar(dst, ps[:, 0:512],
                                        T["coef"][:, tc * 8 + ex: tc * 8 + ex + 1],
                                        None, AOT.mult)
                    else:
                        e.scalar_tensor_tensor(dst, ps[:, 0:512],
                                               T["coef"][:, tc * 8 + ex: tc * 8 + ex + 1],
                                               dst, AOT.mult, AOT.add)
                P.on("vector", ev)
                evS.inc()
                P.on("vector", lambda e: e.drain().then_inc(evS.h, 1))
        econs.inc()
        P.on("tensor", lambda e: e.drain().then_inc(econs.h, 1))
    P.barrier(("tensor", "vector", "scalar", "sync"))

    xbuf = Bb[:, 8192:12288]
    for src_d, cons_nm in ((y1_dram, "l1f_ycons"), (x3_dram, "l1f_xcons")):
        cons = P.aux_sem(cons_nm)
        for tc in range(TC):
            par = tc % 2
            dXp = P.dma_sem(f"d{cons_nm}p{par}")
            if tc >= 2:
                P.on("sync", lambda e, v=tc - 1, cons=cons: e.wait_ge(cons.h, v))
            def fx(e, tc=tc, par=par, src_d=src_d, dXp=dXp):
                e.dma_start(out=xbuf[:, par * D:(par + 1) * D],
                            in_=src_d[tc * 128:(tc + 1) * 128, :]).then_inc(dXp.h, 16)
            P.on("sync", fx)
            nx = dXp.inc(16)
            def va(e, tc=tc, par=par, nx=nx, dXp=dXp):
                e.wait_ge(dXp.h, nx)
                e.tensor_tensor(acc[:, tc * D:(tc + 1) * D], acc[:, tc * D:(tc + 1) * D],
                                xbuf[:, par * D:(par + 1) * D], AOT.add)
            P.on("vector", va)
            cons.inc()
            P.on("vector", lambda e, cons=cons: e.drain().then_inc(cons.h, 1))
        if cons_nm == "l1f_ycons":
            nln = load_ln_tiles(T["ln2g_d"], T["ln2b_d"])
            P.on("vector", lambda e, n=nln: e.wait_ge(lnld.h, n))
            emit_ln(P, acc, acc, T["scr_a"], ms, T["scr_b"],
                    gamma=Bb[:, 0:1024], beta=Bb[:, 1024:2048], nchunk=TC)
        else:
            emit_ln(P, acc, acc, T["scr_a"], ms, T["scr_b"], nchunk=TC)
        P.barrier(("vector", "scalar", "sync"))
    blnf = P.barrier(("vector", "scalar"))
    P.wait_S("gpsimd", blnf)
    emit_store_tokmajor(P, out_dram, acc, "dOutL1f", eng="gpsimd")
    n = P.dma["dOutL1f"].n
    P.on("gpsimd", lambda e, n=n: e.wait_ge(P.dma["dOutL1f"].h, n))
    P.signal_S("gpsimd")


def build_l1ffn_test(debug=False):
    nc = bass.Bass("TRN2", target_bir_lowering=False, detect_race_conditions=False)
    T = {}
    if debug:
        T["dbg_route"] = nc.dram_tensor("dbg_route", [512, 32], F32,
                                        kind="ExternalOutput").ap()
    x3_in = nc.dram_tensor("x3_in", [TOK, D], F32, kind="ExternalInput").ap()
    out_d = nc.dram_tensor("x4_out", [TOK, D], F32, kind="ExternalOutput").ap()
    T["y1_dram"] = nc.dram_tensor("y1_scr", [TOK, D], F32).ap()
    for nm, shp in [("w1T_d", [4096, 1024]), ("w2_d", [4096, 1024]),
                    ("l1f_wT_d", [1024, 3072]), ("l1f_owT_d", [1024, 1024]),
                    ("ln1g_d", [128, 1024]), ("ln1b_d", [128, 1024]),
                    ("ln2g_d", [128, 1024]), ("ln2b_d", [128, 1024])]:
        T[nm] = nc.dram_tensor(nm, shp, F32, kind="ExternalInput").ap()
    ident_d = nc.dram_tensor("ident_c", [128, 128], F32, kind="ExternalInput").ap()
    eps_d = nc.dram_tensor("eps_c", [128, 1], F32, kind="ExternalInput").ap()
    ones_d = nc.dram_tensor("ones_c", [1, 128], F32, kind="ExternalInput").ap()
    b1col_d = nc.dram_tensor("b1col_c", [128, 32], F32, kind="ExternalInput").ap()
    b2row_d = nc.dram_tensor("b2row_c", [1, 1024], F32, kind="ExternalInput").ap()
    brow_d = nc.dram_tensor("brow_c", [1, 3072], F32, kind="ExternalInput").ap()
    obrow_d = nc.dram_tensor("obrow_c", [1, 1024], F32, kind="ExternalInput").ap()
    wg_d = nc.dram_tensor("wg_c", [1024, 8], F32, kind="ExternalInput").ap()
    iota_d = nc.dram_tensor("iota8_c", [128, 8], F32, kind="ExternalInput").ap()
    rev_d = nc.dram_tensor("reviota_c", [128, 8], F32, kind="ExternalInput").ap()
    tri_d = nc.dram_tensor("trimat_c", [128, 256], BF16, kind="ExternalInput").ap()
    mm_d = nc.dram_tensor("moemask_c", [1, 1024], F32, kind="ExternalInput").ap()
    T["ww_d"] = nc.dram_tensor("ww_scr", [1025, 1024], F32).ap()
    T["bb_d"] = nc.dram_tensor("bb_scr", [1, 1024], F32).ap()
    T["ww_all_d"] = nc.dram_tensor("ww_all", [8, 1025, 1024], F32).ap()
    T["bb_all_d"] = nc.dram_tensor("bb_all", [8, 1, 1024], F32).ap()
    T["cnt_d"] = nc.dram_tensor("cnt_scr", [4, 8], F32).ap()
    T["cnt_all_d"] = nc.dram_tensor("cnt_all", [8, 4, 8], F32).ap()
    T["base_d"] = nc.dram_tensor("base_scr", [32], F32).ap()

    T["psum"] = [nc.alloc_psum_tensor(f"pb{i}", [128, 512], F32) for i in range(8)]
    T["A"] = nc.alloc_sbuf_tensor("arenaA", [128, 16384], F32)
    T["B"] = nc.alloc_sbuf_tensor("arenaB", [128, 16384], F32)
    T["y1T"] = nc.alloc_sbuf_tensor("y1T", [128, 16384], F32)
    T["foldout"] = T["A"][:, 0:8192]
    T["ident"] = nc.alloc_sbuf_tensor("ident", [128, 128], F32)
    T["ones_row"] = nc.alloc_sbuf_tensor("ones_row", [1, 128], F32)
    T["epscol"] = nc.alloc_sbuf_tensor("epscol", [128, 1], F32)
    T["mean_s"] = nc.alloc_sbuf_tensor("mean_s", [128, 64], F32)
    T["scr_a"] = nc.alloc_sbuf_tensor("scr_gx", [128, 128], F32)
    T["scr_b"] = T["B"][:, 2048:3072]
    T["scores"] = nc.alloc_sbuf_tensor("scores", [128, 64], F32)
    T["logits"] = nc.alloc_sbuf_tensor("logits", [128, 128], F32)
    T["onehot"] = nc.alloc_sbuf_tensor("onehot", [128, 128], F32)
    T["onehot_bf"] = nc.alloc_sbuf_tensor("onehot_bf", [128, 128], BF16)
    T["gatekeep"] = nc.alloc_sbuf_tensor("gatekeep", [128, 16], F32)
    T["cum"] = nc.alloc_sbuf_tensor("cum", [128, 128], F32)
    T["coef"] = nc.alloc_sbuf_tensor("coef", [128, 128], F32)
    T["basebc"] = nc.alloc_sbuf_tensor("basebc", [128, 32], F32)
    T["cntrow"] = T["B"][0:1, 3072:3648]
    T["bbbuf"] = T["B"][0:1, 8192:9216]
    T["bbrow"] = T["bbbuf"]
    T["b2row"] = T["B"][0:1, 5120:6144]
    T["b1col"] = nc.alloc_sbuf_tensor("b1col", [128, 32], F32)
    T["wg_sb"] = nc.alloc_sbuf_tensor("wg_sb", [128, 64], F32)
    T["iota8"] = nc.alloc_sbuf_tensor("iota8", [128, 8], F32)
    T["reviota"] = nc.alloc_sbuf_tensor("reviota", [128, 8], F32)
    T["trimat"] = nc.alloc_sbuf_tensor("trimat", [128, 256], BF16)
    T["moemask"] = T["B"][0:1, 3648:4672]
    T["brow_slot"] = nc.alloc_sbuf_tensor("brow_slot", [1, 512], F32)
    for nm, dd in [("ident", ident_d), ("ones_row", ones_d), ("epscol", eps_d),
                   ("b1col", b1col_d), ("b2row", b2row_d), ("iota8", iota_d),
                   ("reviota", rev_d), ("trimat", tri_d), ("moemask", mm_d)]:
        T[nm + "_dram"] = dd
    T["l1f_brow_d"] = brow_d
    T["l1f_obrow_d"] = obrow_d

    P = Prog(nc)
    P.epscol = T["epscol"]
    _load_const(P, T, ["ident", "ones_row", "epscol", "b1col", "b2row",
                       "iota8", "reviota", "trimat"])
    dw = P.dma_sem("dWg")
    def fwg(e):
        e.dma_start(out=T["wg_sb"][:, 0:64].rearrange("p (c m) -> p c m", c=DC),
                    in_=wg_d.rearrange("(c p) m -> p c m", p=128)).then_inc(dw.h, 16)
    P.on("sync", fwg)
    P.on("sync", lambda e, n=dw.inc(16): e.wait_ge(dw.h, n))
    P.signal_S("sync")
    emit_l1f_fold(P, T)
    P.barrier(("sync", "tensor", "vector", "scalar", "gpsimd"))
    emit_l1ffn(P, T, x3_in, out_d, T["y1_dram"])
    with nc.Block() as block:
        P.finalize(block)
    return nc


def l0a_host_weights(Wq, bq, Wk, bk, Wv, bv, Wo, bo):
    def stat(W):
        WT = np.ascontiguousarray(W.T.astype(np.float32))
        return np.ascontiguousarray(
            np.stack([WT[:, oc * 128:(oc + 1) * 128] for oc in range(DC)], 0))

    def bcol(bb):
        return np.ascontiguousarray(bb.astype(np.float32).reshape(DC, 128).T)
    wo_T = np.ascontiguousarray(Wo.T.astype(np.float32))
    return {
        "l0a_wq_d": stat(Wq), "l0a_wk_d": stat(Wk), "l0a_wv_d": stat(Wv),
        "l0a_wo_d": np.ascontiguousarray(wo_T.reshape(DC, 128, D)),
        "l0a_bo_d": np.ascontiguousarray(bo.astype(np.float32).reshape(1, D)),
        "bq_sb_c": bcol(bq), "bk_sb_c": bcol(bk), "bv_sb_c": bcol(bv),
    }


def l0a_consts():
    sel4 = np.zeros((128, 16), np.float32)
    for b_ in range(B):
        sel4[:, b_ * 4 + b_] = 1.0
    selpair = np.zeros((16, DC * 128), np.float32)
    for dc_ in range(DC):
        for m in range(128):
            selpair[2 * dc_ + m // 64, dc_ * 128 + m] = 1.0
    return {
        "sel4_c": sel4, "selpair_c": selpair,
        "ident_c": np.eye(128, dtype=np.float32),
        "ones_row_c": np.ones((1, 128), np.float32),
        "epscol_c": np.full((128, 1), 1e-5, np.float32),
    }


def _l0a_alloc(nc, T):
    T["QA"] = nc.alloc_sbuf_tensor("l0aQA", [128, 16384], F32)
    T["KA"] = nc.alloc_sbuf_tensor("l0aKA", [128, 16384], F32)
    T["VA"] = nc.alloc_sbuf_tensor("l0aVA", [128, 4096], F32)
    T["VA2"] = nc.alloc_sbuf_tensor("l0aVA2", [128, 4096], F32)
    T["scrD"] = nc.alloc_sbuf_tensor("l0ascrD", [128, 2048], F32)
    T["kvt"] = nc.alloc_sbuf_tensor("l0akvt", [128, 1024], F32)
    T["ksum4"] = nc.alloc_sbuf_tensor("l0aksum4", [4, 1024], F32)
    T["qsum4"] = nc.alloc_sbuf_tensor("l0aqsum4", [4, 1024], F32)
    T["bd"] = nc.alloc_sbuf_tensor("l0abd", [128, 512], F32)
    T["so_cols"] = nc.alloc_sbuf_tensor("l0aso", [128, 256], F32)
    T["e_cols"] = nc.alloc_sbuf_tensor("l0aec", [128, 256], F32)
    T["ksum_cols"] = nc.alloc_sbuf_tensor("l0aksc", [128, 32], F32)
    T["qsum_cols"] = nc.alloc_sbuf_tensor("l0aqsc", [128, 32], F32)
    T["qsi_cols"] = nc.alloc_sbuf_tensor("l0aqic", [128, 32], F32)
    T["eps416"] = nc.alloc_sbuf_tensor("l0aeps416", [4, 64], F32)
    T["epsT"] = nc.alloc_sbuf_tensor("l0aepsT", [16, 16], F32)
    T["z4"] = nc.alloc_sbuf_tensor("l0az4", [4, 16], F32)
    T["epsq_bc"] = nc.alloc_sbuf_tensor("l0aepsqbc", [128, 16], F32)
    T["mean_s"] = nc.alloc_sbuf_tensor("mean_s", [128, 64], F32)
    T["ident"] = nc.alloc_sbuf_tensor("ident", [128, 128], F32)
    T["ones_row"] = nc.alloc_sbuf_tensor("ones_row", [1, 128], F32)
    T["epscol"] = nc.alloc_sbuf_tensor("epscol", [128, 1], F32)
    T["sel4"] = nc.alloc_sbuf_tensor("sel4", [128, 16], F32)
    T["selpair"] = nc.alloc_sbuf_tensor("selpair", [16, 1024], F32)
    T["bq_sb"] = nc.alloc_sbuf_tensor("bq_sb", [128, 8], F32)
    T["bk_sb"] = nc.alloc_sbuf_tensor("bk_sb", [128, 8], F32)
    T["bv_sb"] = nc.alloc_sbuf_tensor("bv_sb", [128, 8], F32)


def _l0a_dram(nc, T):
    for nm, shp in [("l0a_wq_d", [DC, D, 128]), ("l0a_wk_d", [DC, D, 128]),
                    ("l0a_wv_d", [DC, D, 128]), ("l0a_wo_d", [DC, 128, D]),
                    ("l0a_bo_d", [1, D])]:
        T[nm] = nc.dram_tensor(nm, shp, F32, kind="ExternalInput").ap()
    T["l0a_v_d"] = nc.dram_tensor("l0a_v_scr", [TOK, D], F32).ap()
    for nm in ("l0a_ar1_d", "l0a_ar1o_d", "l0a_ar2_d", "l0a_ar2o_d"):
        T[nm] = nc.dram_tensor(nm, [8, 1024], F32).ap()
    T["l0a_kv_d"] = nc.dram_tensor("l0a_kv_d", [129, 2048], F32).ap()
    T["l0a_kvo_d"] = nc.dram_tensor("l0a_kvo_d", [129, 2048], F32).ap()
    T["l0a_epsq_d"] = nc.dram_tensor("l0a_epsq_d", [4, 16], F32).ap()


L0A_CONST_NAMES = ["ident", "ones_row", "epscol", "sel4", "selpair",
                   "bq_sb", "bk_sb", "bv_sb"]


def build_l0attn_test(debug=False):
    nc = bass.Bass("TRN2", target_bir_lowering=False, detect_race_conditions=False)
    T = {}
    x_in = nc.dram_tensor("x_in", [TOK, D], F32, kind="ExternalInput").ap()
    xT_in = nc.dram_tensor("xT_in", [D, TOK], F32, kind="ExternalInput").ap()
    out_d = nc.dram_tensor("x1_out", [TOK, D], F32, kind="ExternalOutput").ap()
    _l0a_dram(nc, T)
    shp = {"ident": [128, 128], "ones_row": [1, 128], "epscol": [128, 1],
           "sel4": [128, 16], "selpair": [16, 1024],
           "bq_sb": [128, 8], "bk_sb": [128, 8], "bv_sb": [128, 8]}
    for nm in L0A_CONST_NAMES:
        T[nm + "_dram"] = nc.dram_tensor(nm + "_c", shp[nm], F32,
                                         kind="ExternalInput").ap()
    if debug:
        taps = {}
        for nm, s in [("ke", [4, 1024]), ("qe", [4, 1024]),
                      ("si", [16, 2048]), ("so", [128, 256]),
                      ("sa", [16, 2048]), ("e", [128, 256]),
                      ("kv", [128, 4096]), ("z", [4, 16]),
                      ("o", [128, 2048]), ("c", [16, 2048])]:
            taps[nm] = nc.dram_tensor("dbg_" + nm, s, F32,
                                      kind="ExternalOutput").ap()
        T["l0a_taps"] = taps
    T["psum"] = [nc.alloc_psum_tensor(f"pb{i}", [128, 512], F32) for i in range(8)]
    _l0a_alloc(nc, T)
    P = Prog(nc)
    P.epscol = T["epscol"]
    _load_const(P, T, L0A_CONST_NAMES)
    emit_l0attn(P, T, x_in, xT_in, out_d)
    with nc.Block() as block:
        P.finalize(block)
    return nc


def _alloc_common(nc):
    T = {}
    T["psum"] = [nc.alloc_psum_tensor(f"pb{i}", [128, 512], F32) for i in range(8)]
    T["small"] = nc.alloc_sbuf_tensor("small", [128, 2048], F32)
    T["scr_a"] = nc.alloc_sbuf_tensor("scr_a", [128, 1024], F32)
    T["scr_b"] = nc.alloc_sbuf_tensor("scr_b", [128, 1024], F32)
    T["scr_c"] = nc.alloc_sbuf_tensor("scr_c", [128, 2048], F32)
    T["qi32"] = nc.alloc_sbuf_tensor("qi32", [128, 1024], I32)
    T["ident"] = nc.alloc_sbuf_tensor("ident", [128, 128], F32)
    T["ones_row"] = nc.alloc_sbuf_tensor("ones_row", [1, 128], F32)
    T["mean_s"] = nc.alloc_sbuf_tensor("mean_s", [128, 64], F32)
    T["epscol"] = nc.alloc_sbuf_tensor("epscol", [128, 1], F32)
    return T


def _load_const(P, T, names):
    s = P.dma_sem("dConst")
    for nm in names:
        def f(e, nm=nm, s=s):
            e.dma_start(out=T[nm][:, :], in_=T[nm + "_dram"]).then_inc(s.h, 16)
        P.on("sync", f)
        s.inc(16)
    n = s.n
    P.on("sync", lambda e, n=n: e.wait_ge(P.dma["dConst"].h, n))
    P.signal_S("sync")


def build_l1attn_test(batches=tuple(range(B))):
    nc = bass.Bass("TRN2", target_bir_lowering=False, detect_race_conditions=False)
    T = {}
    T["x2_dram"] = nc.dram_tensor("x2_in", [TOK, D], F32, kind="ExternalInput").ap()
    T["wqT_d"] = nc.dram_tensor("wqT", [D, D], F32, kind="ExternalInput").ap()
    T["wkT_d"] = nc.dram_tensor("wkT", [D, D], F32, kind="ExternalInput").ap()
    T["wvT_d"] = nc.dram_tensor("wvT", [D, D], F32, kind="ExternalInput").ap()
    T["fre_d"] = nc.dram_tensor("fre", [D, BW], F32, kind="ExternalInput").ap()
    T["fim_d"] = nc.dram_tensor("fim", [D, BW], F32, kind="ExternalInput").ap()
    T["irre_d"] = nc.dram_tensor("irre", [BW, D], F32, kind="ExternalInput").ap()
    T["irim_d"] = nc.dram_tensor("irim", [BW, D], F32, kind="ExternalInput").ap()
    ident_d = nc.dram_tensor("ident_c", [128, 128], F32, kind="ExternalInput").ap()
    eps_d = nc.dram_tensor("eps_c", [128, 1], F32, kind="ExternalInput").ap()
    cm_d = nc.dram_tensor("coremask80_c", [128, 80], F32, kind="ExternalInput").ap()
    x3_out = nc.dram_tensor("x3_out", [TOK, D], F32, kind="ExternalOutput").ap()
    T["ag_in_d"] = nc.dram_tensor("ag_in", [128, 10], F32).ap()
    T["ag_out_d"] = nc.dram_tensor("ag_out", [8, 128, 10], F32).ap()

    T["psum"] = [nc.alloc_psum_tensor(f"pb{i}", [128, 512], F32) for i in range(8)]
    T["ident"] = nc.alloc_sbuf_tensor("ident", [128, 128], F32)
    T["mean_s"] = nc.alloc_sbuf_tensor("mean_s", [128, 64], F32)
    T["epscol"] = nc.alloc_sbuf_tensor("epscol", [128, 1], F32)
    T["warena"] = nc.alloc_sbuf_tensor("warena", [128, 10240], F32)
    T["ir_re"] = T["warena"]
    T["ir_im"] = nc.alloc_sbuf_tensor("ir_im", [128, NB5 * D], F32)
    T["fre"] = nc.alloc_sbuf_tensor("freS", [128, DC * 512], F32)
    T["fim"] = nc.alloc_sbuf_tensor("fimS", [128, DC * 512], F32)
    T["fnyq"] = nc.alloc_sbuf_tensor("fnyq", [128, 16], F32)
    T["x2T"] = nc.alloc_sbuf_tensor("x2T", [128, 4096], F32)
    T["x3b"] = nc.virtual_sbuf_tensor("x3b", [128, 4096], F32)
    nc.back_virtual_tensor_with(T["x3b"], T["x2T"])
    T["qkv"] = nc.alloc_sbuf_tensor("qkv", [128, 4096], F32)
    T["xbuf"] = nc.alloc_sbuf_tensor("xbuf", [128, 2048], F32)
    for nm in ("fq_re", "fq_im", "fk_re", "fk_im", "pv_re", "pv_im"):
        T[nm] = nc.alloc_sbuf_tensor(nm, [128, NB5 * 512], F32)
    scr_big = nc.virtual_sbuf_tensor("scr_big", [128, NB5 * D], F32)
    nc.back_virtual_tensor_with(scr_big, T["ir_im"])
    T["scr_c"] = scr_big[:, 0:2048]
    T["scr_spare2"] = scr_big[:, 2048:4096]
    scr_ab = nc.virtual_sbuf_tensor("scr_ab", [128, NB5 * 512], F32)
    nc.back_virtual_tensor_with(scr_ab, T["fk_re"])
    T["scr_a"] = scr_ab[:, 0:1024]
    scr_bb = nc.virtual_sbuf_tensor("scr_bb", [128, NB5 * 512], F32)
    nc.back_virtual_tensor_with(scr_bb, T["fk_im"])
    T["scr_b"] = scr_bb[:, 0:1024]
    T["totals"] = nc.alloc_sbuf_tensor("totals", [128, 16], F32)
    T["carry"] = nc.alloc_sbuf_tensor("carry", [128, 16], F32)
    T["carry_all"] = nc.alloc_sbuf_tensor("carry_all", [128, 80], F32)
    T["coremask80"] = nc.alloc_sbuf_tensor("coremask80", [128, 80], F32)
    T["ident_dram"] = ident_d
    T["epscol_dram"] = eps_d
    T["coremask80_dram"] = cm_d

    P = Prog(nc)
    P.epscol = T["epscol"]
    _load_const(P, T, ["ident", "epscol", "coremask80"])
    emit_l1attn(P, T, T["x2_dram"], x3_out, batches=batches)
    with nc.Block() as block:
        P.finalize(block)
    return nc


def build_l0ffn_test(debug=False):
    nc = bass.Bass("TRN2", target_bir_lowering=False, detect_race_conditions=False)
    x_in = nc.dram_tensor("x_in", [TOK, D], F32, kind="ExternalInput").ap()
    rhsT = nc.dram_tensor("rhsT", [D, D], F32, kind="ExternalInput").ap()
    ident_d = nc.dram_tensor("ident_c", [128, 128], F32, kind="ExternalInput").ap()
    ones_d = nc.dram_tensor("ones_c", [1, 128], F32, kind="ExternalInput").ap()
    x_out = nc.dram_tensor("x_out", [TOK, D], F32, kind="ExternalOutput").ap()

    T = _alloc_common(nc)
    T["ident_dram"] = ident_d
    T["ones_row_dram"] = ones_d
    T["rhsT_dram"] = rhsT
    T["x1"] = nc.alloc_sbuf_tensor("x1", [128, TC * D], F32)
    T["qbf"] = nc.alloc_sbuf_tensor("qbf", [128, TC * D], BF16)
    T["qbfT"] = nc.alloc_sbuf_tensor("qbfT", [128, DC * TOK], BF16)
    T["qrhsT"] = nc.alloc_sbuf_tensor("qrhsT", [128, DC * D], BF16)
    T["qrhs"] = nc.alloc_sbuf_tensor("qrhs", [128, DC * D], BF16)
    T["rhs_scale_bcast"] = nc.alloc_sbuf_tensor("rsb", [128, D], F32)
    T["lscale"] = nc.alloc_sbuf_tensor("lscale", [128, TC], F32)
    T["rs_dram"] = nc.dram_tensor("rs_scratch", [DC, 128], F32).ap()

    eps_d = nc.dram_tensor("eps_c", [128, 1], F32, kind="ExternalInput").ap()
    T["epscol_dram"] = eps_d
    if debug:
        dbg = {}
        for nm, w, dt_ in [("d_qbf", TC * D, BF16), ("d_qbfT", DC * TOK, BF16),
                           ("d_qrhs", DC * D, BF16), ("d_rsb", D, F32),
                           ("d_lsc", TC, F32), ("d_pre", TC * D, F32),
                           ("d_qbfT2", DC * TOK, BF16), ("d_qrhs2", DC * D, BF16),
                           ("d_x1pre", TC * D, F32)]:
            dbg[nm] = nc.dram_tensor(nm, [128, w], dt_, kind="ExternalOutput").ap()
        T["premm_taps"] = [(dbg["d_qbfT2"], T["qbfT"]), (dbg["d_qrhs2"], T["qrhs"]),
                           (dbg["d_x1pre"], T["x1"])]

    P = Prog(nc)
    P.epscol = T["epscol"]
    _load_const(P, T, ["ident", "ones_row", "epscol"])
    nx = emit_load_tokmajor(P, T["x1"], x_in, "dX")
    P.on("vector", lambda e, n=nx: e.wait_ge(P.dma["dX"].h, n))
    P.signal_S("vector")
    emit_l0ffn_prep(P, T)
    P.barrier(("vector", "tensor", "sync", "scalar"))
    emit_l0ffn(P, T)
    if debug:
        v = P.barrier(("vector", "scalar", "tensor"))
        P.wait_S("gpsimd", v)
        dd = P.dma_sem("dDbg")
        for nm, src in [("d_qbf", T["qbf"]), ("d_qbfT", T["qbfT"]),
                        ("d_qrhs", T["qrhs"]), ("d_rsb", T["rhs_scale_bcast"]),
                        ("d_lsc", T["lscale"]), ("d_pre", T["x1"])]:
            def f(e, nm=nm, src=src, dd=dd):
                e.dma_start(out=dbg[nm], in_=src[:, :]).then_inc(dd.h, 16)
            P.on("gpsimd", f)
            dd.inc(16)
        P.on("gpsimd", lambda e, n=dd.n: e.wait_ge(dd.h, n))
        P.signal_S("gpsimd")
    emit_ln(P, T["x1"], T["x1"], T["scr_a"], T["mean_s"], T["scr_b"])
    v = P.barrier(("vector", "scalar"))
    P.wait_S("gpsimd", v)
    emit_store_tokmajor(P, x_out, T["x1"], "dOut", eng="gpsimd")
    n = P.dma["dOut"].n
    P.on("gpsimd", lambda e, n=n: e.wait_ge(P.dma["dOut"].h, n))
    with nc.Block() as block:
        P.finalize(block)
    return nc


def _to_np(a, dtype=np.float32):
    return np.ascontiguousarray(np.asarray(a, dtype=dtype))


def stage_core_slices(x_full):
    out = []
    for c in range(NCORES):
        xc = x_full[:, c * SLOC:(c + 1) * SLOC, :].reshape(TOK, D)
        out.append(_to_np(xc))
    return out


def build_l0ffn_test2():
    nc = bass.Bass("TRN2", target_bir_lowering=False, detect_race_conditions=False)
    T = {}
    x_in = nc.dram_tensor("x_in", [TOK, D], F32, kind="ExternalInput").ap()
    rhsT = nc.dram_tensor("rhsT", [D, D], F32, kind="ExternalInput").ap()
    ident_d = nc.dram_tensor("ident_c", [128, 128], F32, kind="ExternalInput").ap()
    eps_d = nc.dram_tensor("eps_c", [128, 1], F32, kind="ExternalInput").ap()
    x_out = nc.dram_tensor("x_out", [TOK, D], F32, kind="ExternalOutput").ap()
    T["rhsT_dram"] = rhsT
    T["rs_dram"] = nc.dram_tensor("rs_scratch", [DC, 128], F32).ap()

    T["psum"] = [nc.alloc_psum_tensor(f"pb{i}", [128, 512], F32) for i in range(8)]
    A1 = nc.alloc_sbuf_tensor("fA1", [128, 16384], F32)
    A2 = nc.alloc_sbuf_tensor("fA2", [128, 16384], F32)
    A3 = nc.alloc_sbuf_tensor("fA3", [128, 16384], F32)
    T["x1"] = A1
    T["acc"] = A3
    T["qrhs"] = nc.alloc_sbuf_tensor("qrhs", [128, 2048], BF16)
    T["rbuf"] = A2[:, 0:2048]
    T["scr_a"] = A2[:, 2048:3072]
    T["scr_b"] = A2[:, 3072:4096]
    T["small"] = A2[:, 4096:6144]
    T["rhs_scale_bcast"] = A2[:, 6144:7168]
    T["lscale"] = A2[:, 7168:7184]
    T["bt"] = nc.alloc_sbuf_tensor("bt", [128, 256], BF16)
    T["qi32"] = nc.alloc_sbuf_tensor("qi32", [128, 512], I32)
    T["ident"] = nc.alloc_sbuf_tensor("ident", [128, 128], F32)
    T["epscol"] = nc.alloc_sbuf_tensor("epscol", [128, 1], F32)
    T["mean_s"] = nc.alloc_sbuf_tensor("mean_s", [128, 64], F32)
    T["ident_dram"] = ident_d
    T["epscol_dram"] = eps_d

    P = Prog(nc)
    P.epscol = T["epscol"]
    _load_const(P, T, ["ident", "epscol"])
    nx = emit_load_tokmajor(P, T["x1"], x_in, "dX")
    P.on("vector", lambda e, n=nx: e.wait_ge(P.dma["dX"].h, n))
    P.signal_S("vector")
    emit_l0ffn_fused(P, T)
    emit_ln(P, T["x1"], T["x1"], T["scr_a"], T["mean_s"], T["scr_b"])
    v = P.barrier(("vector", "scalar"))
    P.wait_S("gpsimd", v)
    emit_store_tokmajor(P, x_out, T["x1"], "dOut", eng="gpsimd")
    n = P.dma["dOut"].n
    P.on("gpsimd", lambda e, n=n: e.wait_ge(P.dma["dOut"].h, n))
    with nc.Block() as block:
        P.finalize(block)
    return nc


def build_fused():
    nc = bass.Bass("TRN2", target_bir_lowering=False, detect_race_conditions=False)
    T = {}
    x_in = nc.dram_tensor("x_in", [TOK, D], F32, kind="ExternalInput").ap()
    xT_in = nc.dram_tensor("xT_in", [D, TOK], F32, kind="ExternalInput").ap()
    out_d = nc.dram_tensor("x4_out", [TOK, D], F32, kind="ExternalOutput").ap()
    x1_d = nc.dram_tensor("x1_scr", [TOK, D], F32).ap()
    x2_d = nc.dram_tensor("x2_scr", [TOK, D], F32).ap()
    x3_d = nc.dram_tensor("x3_scr", [TOK, D], F32).ap()
    _l0a_dram(nc, T)
    shp = {"ident": [128, 128], "ones_row": [1, 128], "epscol": [128, 1],
           "sel4": [128, 16], "selpair": [16, 1024],
           "bq_sb": [128, 8], "bk_sb": [128, 8], "bv_sb": [128, 8]}
    for nm in L0A_CONST_NAMES:
        T[nm + "_dram"] = nc.dram_tensor(nm + "_c", shp[nm], F32,
                                         kind="ExternalInput").ap()
    T["rhsT_dram"] = nc.dram_tensor("rhsT", [D, D], F32, kind="ExternalInput").ap()
    T["rs_dram"] = nc.dram_tensor("rs_scratch", [DC, 128], F32).ap()
    T["wqT_d"] = nc.dram_tensor("wqT", [D, D], F32, kind="ExternalInput").ap()
    T["wkT_d"] = nc.dram_tensor("wkT", [D, D], F32, kind="ExternalInput").ap()
    T["wvT_d"] = nc.dram_tensor("wvT", [D, D], F32, kind="ExternalInput").ap()
    T["fre_d"] = nc.dram_tensor("fre", [D, BW], F32, kind="ExternalInput").ap()
    T["fim_d"] = nc.dram_tensor("fim", [D, BW], F32, kind="ExternalInput").ap()
    T["irre_d"] = nc.dram_tensor("irre", [BW, D], F32, kind="ExternalInput").ap()
    T["irim_d"] = nc.dram_tensor("irim", [BW, D], F32, kind="ExternalInput").ap()
    T["coremask80_dram"] = nc.dram_tensor("coremask80_c", [128, 80], F32,
                                          kind="ExternalInput").ap()
    T["ag_in_d"] = nc.dram_tensor("ag_in", [128, 10], F32).ap()
    T["ag_out_d"] = nc.dram_tensor("ag_out", [8, 128, 10], F32).ap()
    for nm, s in [("w1T_d", [4096, 1024]), ("w2_d", [4096, 1024]),
                  ("l1f_wT_d", [1024, 3072]), ("l1f_owT_d", [1024, 1024]),
                  ("ln1g_d", [128, 1024]), ("ln1b_d", [128, 1024]),
                  ("ln2g_d", [128, 1024]), ("ln2b_d", [128, 1024])]:
        T[nm] = nc.dram_tensor(nm, s, F32, kind="ExternalInput").ap()
    b1col_d = nc.dram_tensor("b1col_c", [128, 32], F32, kind="ExternalInput").ap()
    b2row_d = nc.dram_tensor("b2row_c", [1, 1024], F32, kind="ExternalInput").ap()
    brow_d = nc.dram_tensor("brow_c", [1, 3072], F32, kind="ExternalInput").ap()
    obrow_d = nc.dram_tensor("obrow_c", [1, 1024], F32, kind="ExternalInput").ap()
    wg_d = nc.dram_tensor("wg_c", [1024, 8], F32, kind="ExternalInput").ap()
    iota_d = nc.dram_tensor("iota8_c", [128, 8], F32, kind="ExternalInput").ap()
    rev_d = nc.dram_tensor("reviota_c", [128, 8], F32, kind="ExternalInput").ap()
    tri_d = nc.dram_tensor("trimat_c", [128, 256], BF16, kind="ExternalInput").ap()
    mm_d = nc.dram_tensor("moemask_c", [1, 1024], F32, kind="ExternalInput").ap()
    T["ww_d"] = nc.dram_tensor("ww_scr", [1025, 1024], F32).ap()
    T["ww_all_d"] = nc.dram_tensor("ww_all", [8, 1025, 1024], F32).ap()
    T["cnt_d"] = nc.dram_tensor("cnt_scr", [4, 8], F32).ap()
    T["cnt_all_d"] = nc.dram_tensor("cnt_all", [8, 4, 8], F32).ap()
    T["base_d"] = nc.dram_tensor("base_scr", [32], F32).ap()
    T["y1_dram"] = nc.dram_tensor("y1_scr", [TOK, D], F32).ap()
    T["l1f_brow_d"] = brow_d
    T["l1f_obrow_d"] = obrow_d

    T["psum"] = [nc.alloc_psum_tensor(f"pb{i}", [128, 512], F32) for i in range(8)]
    A1 = nc.alloc_sbuf_tensor("fA1", [128, 16384], F32)
    A2 = nc.alloc_sbuf_tensor("fA2", [128, 16384], F32)
    A3 = nc.alloc_sbuf_tensor("fA3", [128, 16384], F32)
    T["qrhs"] = nc.alloc_sbuf_tensor("qrhs", [128, 2048], BF16)
    T["bt"] = nc.alloc_sbuf_tensor("bt", [128, 256], BF16)
    T["qi32"] = nc.alloc_sbuf_tensor("qi32", [128, 512], I32)
    T["ident"] = nc.alloc_sbuf_tensor("ident", [128, 128], F32)
    T["epscol"] = nc.alloc_sbuf_tensor("epscol", [128, 1], F32)
    T["ones_row"] = nc.alloc_sbuf_tensor("ones_row", [1, 128], F32)
    T["mean_s"] = nc.alloc_sbuf_tensor("mean_s", [128, 64], F32)
    T["trimat"] = nc.alloc_sbuf_tensor("trimat", [128, 256], BF16)
    T["onehot_bf"] = nc.alloc_sbuf_tensor("onehot_bf", [128, 128], BF16)
    T["iota8"] = nc.alloc_sbuf_tensor("iota8", [128, 8], F32)
    T["reviota"] = nc.alloc_sbuf_tensor("reviota", [128, 8], F32)
    T["b1col"] = nc.alloc_sbuf_tensor("b1col", [128, 32], F32)
    T["wg_sb"] = nc.alloc_sbuf_tensor("wg_sb", [128, 64], F32)
    T["brow_slot"] = nc.alloc_sbuf_tensor("brow_slot", [1, 512], F32)
    T["basebc"] = nc.alloc_sbuf_tensor("basebc", [128, 32], F32)
    T["logits"] = nc.alloc_sbuf_tensor("sm_lg", [128, 128], F32)
    T["onehot"] = nc.alloc_sbuf_tensor("sm_oh", [128, 128], F32)
    T["cum"] = nc.alloc_sbuf_tensor("sm_cum", [128, 128], F32)
    T["coef"] = nc.alloc_sbuf_tensor("sm_coef", [128, 128], F32)
    T["sm_gx"] = nc.alloc_sbuf_tensor("sm_gx", [128, 128], F32)
    T["scores"] = nc.alloc_sbuf_tensor("sm_sc", [128, 64], F32)
    T["gatekeep"] = nc.alloc_sbuf_tensor("sm_gk", [128, 16], F32)

    P = Prog(nc)
    P.epscol = T["epscol"]

    T.update({
        "QA": A1, "KA": A2,
        "VA": A3[:, 0:4096], "VA2": A3[:, 4096:8192],
        "scrD": A3[:, 8192:10240], "kvt": A3[:, 10240:11264],
        "selpair": A3[0:16, 11264:12288],
        "ksum4": A3[0:4, 12288:13312], "qsum4": A3[0:4, 13312:14336],
        "bd": A3[:, 14336:14848],
        "so_cols": A3[:, 14848:15104], "e_cols": A3[:, 15104:15360],
        "ksum_cols": A3[:, 15360:15392], "qsum_cols": A3[:, 15392:15424],
        "qsi_cols": A3[:, 15424:15456],
        "eps416": A3[0:4, 15456:15520], "epsT": A3[0:16, 15520:15536],
        "z4": A3[0:4, 15536:15552], "epsq_bc": A3[:, 15552:15568],
        "sel4": A3[:, 15568:15584],
        "bq_sb": A3[:, 15584:15592], "bk_sb": A3[:, 15592:15600],
        "bv_sb": A3[:, 15600:15608],
    })
    T["l0a_x1_resident"] = A1
    T["l0a_wo_slots"] = ([A3[:, 4096 + i * 1024:4096 + (i + 1) * 1024]
                          for i in range(4)] +
                         [A3[:, 10240 + i * 1024:10240 + (i + 1) * 1024]
                          for i in range(4)])
    T["l0a_bo_slot"] = A3[0:1, 14336:15360]
    _load_const(P, T, L0A_CONST_NAMES)
    emit_l0attn(P, T, x_in, xT_in, x1_d)
    P.barrier(ENGINES)

    T.update({
        "x1": A1, "acc": A3,
        "rbuf": A2[:, 0:2048], "scr_a": A2[:, 2048:3072],
        "scr_b": A2[:, 3072:4096], "small": A2[:, 4096:6144],
        "rhs_scale_bcast": A2[:, 6144:7168], "lscale": A2[:, 7168:7184],
    })
    emit_l0ffn_fused(P, T)
    emit_ln(P, A1, A1, T["scr_a"], T["mean_s"], T["scr_b"])
    v = P.barrier(("vector", "scalar"))
    P.wait_S("gpsimd", v)
    emit_store_tokmajor(P, x2_d, A1, "dX2s", eng="gpsimd")
    P.on("gpsimd", lambda e, n=P.dma["dX2s"].n: e.wait_ge(P.dma["dX2s"].h, n))
    P.signal_S("gpsimd")
    P.barrier(ENGINES)

    T.update({
        "warena": A1[:, 0:8192], "ir_re": A1[:, 0:5120],
        "ir_im": A1[:, 8192:13312],
        "scr_c": A1[:, 8192:10240], "scr_spare2": A1[:, 10240:12288],
        "xbuf": A1[:, 13312:15360],
        "fre": A2[:, 0:4096], "fim": A2[:, 4096:8192],
        "x2T": A2[:, 8192:12288], "x3b": A2[:, 8192:12288],
        "qkv": A2[:, 12288:16384],
        "fq_re": A3[:, 0:2560], "fq_im": A3[:, 2560:5120],
        "fk_re": A3[:, 5120:7680], "fk_im": A3[:, 7680:10240],
        "pv_re": A3[:, 10240:12800], "pv_im": A3[:, 12800:15360],
        "scr_a": A3[:, 5120:6144], "scr_b": A3[:, 7680:8704],
        "totals": A3[:, 15360:15376], "carry": A3[:, 15376:15392],
        "carry_all": A3[:, 15392:15472], "coremask80": A3[:, 15472:15552],
        "fnyq": A3[:, 15552:15568],
    })
    _load_const(P, T, ["coremask80"])
    emit_l1attn(P, T, x2_d, x3_d)
    P.barrier(ENGINES)

    T.update({
        "A": A1, "B": A2, "y1T": A3,
        "foldout": A1[:, 0:8192],
        "scr_a": T["sm_gx"], "scr_b": A2[:, 2048:3072],
        "cntrow": A2[0:1, 3072:3648], "bbbuf": A2[0:1, 8192:9216],
        "bbrow": A2[0:1, 8192:9216], "b2row": A2[0:1, 5120:6144],
        "moemask": A2[0:1, 3648:4672],
        "b1col_dram": b1col_d, "b2row_dram": b2row_d,
        "iota8_dram": iota_d, "reviota_dram": rev_d, "trimat_dram": tri_d,
        "moemask_dram": mm_d,
    })
    _load_const(P, T, ["b1col", "b2row", "iota8", "reviota", "trimat"])
    dw = P.dma_sem("dWg")
    def fwg(e):
        e.dma_start(out=T["wg_sb"][:, 0:64].rearrange("p (c m) -> p c m", c=DC),
                    in_=wg_d.rearrange("(c p) m -> p c m", p=128)).then_inc(dw.h, 16)
    P.on("sync", fwg)
    P.on("sync", lambda e, n=dw.inc(16): e.wait_ge(dw.h, n))
    P.signal_S("sync")
    emit_l1f_fold(P, T)
    P.barrier(ENGINES)
    emit_l1ffn(P, T, x3_d, out_d, T["y1_dram"])
    with nc.Block() as block:
        P.finalize(block)
    return nc


def dft_consts():
    n = np.arange(1024)[:, None].astype(np.float64)
    k = np.arange(513)[None, :].astype(np.float64)
    ang = 2 * np.pi * n * k / 1024.0
    fre = np.cos(ang).astype(np.float32)
    fim = (-np.sin(ang)).astype(np.float32)
    w = np.full((513, 1), 2.0)
    w[0, 0] = 1.0
    w[512, 0] = 1.0
    kk = np.arange(513)[:, None].astype(np.float64)
    nn = np.arange(1024)[None, :].astype(np.float64)
    ang2 = 2 * np.pi * kk * nn / 1024.0
    irre = (w * np.cos(ang2) / 1024.0).astype(np.float32)
    irim = (-w * np.sin(ang2) / 1024.0).astype(np.float32)
    irim[0, :] = 0.0
    irim[512, :] = 0.0
    return fre, fim, irre, irim


def l1f_consts():
    tri = np.triu(np.ones((128, 128), np.float32))
    ones = np.ones((128, 128), np.float32)
    trimat = np.concatenate([tri, ones], axis=1).astype(ml_dtypes.bfloat16)
    iota = np.broadcast_to(np.arange(8, dtype=np.float32), (128, 8)).copy()
    rev = np.broadcast_to(8.0 - np.arange(8, dtype=np.float32), (128, 8)).copy()
    return trimat, iota, rev


def coremask80(c):
    m = np.zeros(80, np.float32)
    for kcore in range(c):
        m[kcore * 10:(kcore + 1) * 10] = 1.0
    return np.broadcast_to(m, (128, 80)).copy()


def moemask(c):
    mm = np.zeros((4, 256), np.float32)
    for b_ in range(4):
        for cp in range(8):
            for bp in range(4):
                if (bp < b_) or (bp == b_ and cp < c):
                    mm[b_, (cp * 4 + bp) * 8:(cp * 4 + bp) * 8 + 8] = 1.0
    return mm.reshape(1, 1024).copy()


_NC_CACHE = {}


def _get_nc(name, fn):
    if name not in _NC_CACHE:
        _NC_CACHE[name] = fn()
    return _NC_CACHE[name]


def kernel(**inputs):
    f32 = np.float32
    gi = lambda nm: np.asarray(inputs[nm], f32)
    x = gi("L0_attn_x")
    cores = list(range(NCORES))

    wmap = l0a_host_weights(gi("L0_attn_Wq"), gi("L0_attn_bq"),
                            gi("L0_attn_Wk"), gi("L0_attn_bk"),
                            gi("L0_attn_Wv"), gi("L0_attn_bv"),
                            gi("L0_attn_Wo"), gi("L0_attn_bo"))
    cmap = l0a_consts()
    rhsT = np.ascontiguousarray(gi("L0_ffn_rhs").T)
    fre, fim, irre, irim = dft_consts()
    wqT = np.ascontiguousarray(gi("L1_attn_Wq").T)
    wkT = np.ascontiguousarray(gi("L1_attn_Wk").T)
    wvT = np.ascontiguousarray(gi("L1_attn_Wv").T)
    trimat, iota, rev = l1f_consts()
    ipw, ipb = gi("L1_ffn_in_proj_w"), gi("L1_ffn_in_proj_b")
    opw, opb = gi("L1_ffn_out_proj_w"), gi("L1_ffn_out_proj_b")
    g1, b1_ = gi("L1_ffn_ln1_g"), gi("L1_ffn_ln1_b")
    g2, b2_ = gi("L1_ffn_ln2_g"), gi("L1_ffn_ln2_b")
    wg = gi("L1_ffn_wg")
    w1, b1e = gi("L1_ffn_w1"), gi("L1_ffn_b1")
    w2, b2e = gi("L1_ffn_w2"), gi("L1_ffn_b2")
    ipwT = np.ascontiguousarray(ipw.T)
    opwT = np.ascontiguousarray(opw.T)

    ncf = _get_nc("fused", build_fused)
    in_maps = []
    for c in cores:
        xc = np.ascontiguousarray(x[:, c * SLOC:(c + 1) * SLOC, :].reshape(TOK, D))
        m = {"x_in": xc, "xT_in": np.ascontiguousarray(xc.T)}
        m.update(wmap)
        m.update(cmap)
        m.update({
            "rhsT": rhsT,
            "wqT": wqT, "wkT": wkT, "wvT": wvT,
            "fre": fre, "fim": fim, "irre": irre, "irim": irim,
            "coremask80_c": coremask80(c),
            "w1T_d": np.ascontiguousarray(w1[c].T),
            "w2_d": np.ascontiguousarray(w2[c]),
            "l1f_wT_d": ipwT, "l1f_owT_d": opwT,
            "ln1g_d": np.broadcast_to(g1, (128, 1024)).copy(),
            "ln1b_d": np.broadcast_to(b1_, (128, 1024)).copy(),
            "ln2g_d": np.broadcast_to(g2, (128, 1024)).copy(),
            "ln2b_d": np.broadcast_to(b2_, (128, 1024)).copy(),
            "b1col_c": np.ascontiguousarray(b1e[c].reshape(32, 128).T),
            "b2row_c": b2e[c].reshape(1, 1024).copy(),
            "brow_c": ipb.reshape(1, 3072).copy(),
            "obrow_c": opb.reshape(1, 1024).copy(),
            "wg_c": wg,
            "iota8_c": iota, "reviota_c": rev, "trimat_c": trimat,
            "moemask_c": moemask(c),
        })
        in_maps.append(m)
    res = run_bass_kernel_spmd(ncf, in_maps, core_ids=cores)
    out = np.zeros((B, S, D), f32)
    for c in cores:
        out[:, c * SLOC:(c + 1) * SLOC, :] = \
            np.asarray(res.results[c]["x4_out"]).reshape(B, SLOC, D)
    return out


def kernel_multilaunch(**inputs):
    f32 = np.float32
    gi = lambda nm: np.asarray(inputs[nm], f32)
    x = gi("L0_attn_x")
    cores = list(range(NCORES))
    ident = np.eye(128, dtype=f32)
    eps = np.full((128, 1), 1e-5, f32)
    ones_r = np.ones((1, 128), f32)

    wmap = l0a_host_weights(gi("L0_attn_Wq"), gi("L0_attn_bq"),
                            gi("L0_attn_Wk"), gi("L0_attn_bk"),
                            gi("L0_attn_Wv"), gi("L0_attn_bv"),
                            gi("L0_attn_Wo"), gi("L0_attn_bo"))
    cmap = l0a_consts()
    nc1 = _get_nc("l0a", build_l0attn_test)
    in1 = []
    for c in cores:
        xc = np.ascontiguousarray(x[:, c * SLOC:(c + 1) * SLOC, :].reshape(TOK, D))
        m = {"x_in": xc, "xT_in": np.ascontiguousarray(xc.T)}
        m.update(wmap)
        m.update(cmap)
        in1.append(m)
    r1 = run_bass_kernel_spmd(nc1, in1, core_ids=cores)
    x1s = [np.asarray(r1.results[c]["x1_out"]) for c in cores]

    nc2 = _get_nc("l0f", build_l0ffn_test)
    rhsT = np.ascontiguousarray(gi("L0_ffn_rhs").T)
    in2 = [{"x_in": x1s[c], "rhsT": rhsT, "ident_c": ident,
            "ones_c": ones_r, "eps_c": eps} for c in cores]
    r2 = run_bass_kernel_spmd(nc2, in2, core_ids=cores)
    x2s = [np.asarray(r2.results[c]["x_out"]) for c in cores]

    nc3 = _get_nc("l1a", build_l1attn_test)
    fre, fim, irre, irim = dft_consts()
    wqT = np.ascontiguousarray(gi("L1_attn_Wq").T)
    wkT = np.ascontiguousarray(gi("L1_attn_Wk").T)
    wvT = np.ascontiguousarray(gi("L1_attn_Wv").T)
    in3 = [{"x2_in": x2s[c], "wqT": wqT, "wkT": wkT, "wvT": wvT,
            "fre": fre, "fim": fim, "irre": irre, "irim": irim,
            "ident_c": ident, "eps_c": eps, "coremask80_c": coremask80(c)}
           for c in cores]
    r3 = run_bass_kernel_spmd(nc3, in3, core_ids=cores)
    x3s = [np.asarray(r3.results[c]["x3_out"]) for c in cores]

    nc4 = _get_nc("l1f", build_l1ffn_test)
    trimat, iota, rev = l1f_consts()
    ipw, ipb = gi("L1_ffn_in_proj_w"), gi("L1_ffn_in_proj_b")
    opw, opb = gi("L1_ffn_out_proj_w"), gi("L1_ffn_out_proj_b")
    g1, b1_ = gi("L1_ffn_ln1_g"), gi("L1_ffn_ln1_b")
    g2, b2_ = gi("L1_ffn_ln2_g"), gi("L1_ffn_ln2_b")
    wg = gi("L1_ffn_wg")
    w1, b1e = gi("L1_ffn_w1"), gi("L1_ffn_b1")
    w2, b2e = gi("L1_ffn_w2"), gi("L1_ffn_b2")
    ipwT = np.ascontiguousarray(ipw.T)
    opwT = np.ascontiguousarray(opw.T)
    in4 = []
    for c in cores:
        in4.append({
            "x3_in": x3s[c],
            "w1T_d": np.ascontiguousarray(w1[c].T),
            "w2_d": np.ascontiguousarray(w2[c]),
            "l1f_wT_d": ipwT, "l1f_owT_d": opwT,
            "ln1g_d": np.broadcast_to(g1, (128, 1024)).copy(),
            "ln1b_d": np.broadcast_to(b1_, (128, 1024)).copy(),
            "ln2g_d": np.broadcast_to(g2, (128, 1024)).copy(),
            "ln2b_d": np.broadcast_to(b2_, (128, 1024)).copy(),
            "ident_c": ident, "eps_c": eps, "ones_c": ones_r,
            "b1col_c": np.ascontiguousarray(b1e[c].reshape(32, 128).T),
            "b2row_c": b2e[c].reshape(1, 1024).copy(),
            "brow_c": ipb.reshape(1, 3072).copy(),
            "obrow_c": opb.reshape(1, 1024).copy(),
            "wg_c": wg,
            "iota8_c": iota, "reviota_c": rev, "trimat_c": trimat,
            "moemask_c": moemask(c),
        })
    r4 = run_bass_kernel_spmd(nc4, in4, core_ids=cores)
    out = np.zeros((B, S, D), f32)
    for c in cores:
        out[:, c * SLOC:(c + 1) * SLOC, :] = \
            np.asarray(r4.results[c]["x4_out"]).reshape(B, SLOC, D)
    return out


if __name__ == "__main__":
    print("kernel.py library module (under construction)")
